# version 35
# baseline (speedup 1.0000x reference)
import numpy as np
import ml_dtypes
from contextlib import ExitStack

import concourse.bass as bass
import concourse.mybir as mybir
from concourse.bass_utils import run_bass_kernel_spmd

F32 = mybir.dt.float32
F16 = mybir.dt.float16
BF16 = mybir.dt.bfloat16
U16 = mybir.dt.uint16
I32 = mybir.dt.int32
ALU = mybir.AluOpType
AF = mybir.ActivationFunctionType
AX = mybir.AxisListType

T = 2048
C = 1024
II = 2816
NT = 16
NCH = 4
CT = 8
IT = 22
CHT = 512
EPS = 1e-5
MAGIC = 1597463007.0

ENGINES = ("sp", "pe", "act", "dve", "pool")


class Prog:

    def __init__(self):
        self.ops = []
        self.eng_ops = {e: [] for e in ENGINES}
        self.writer = {}
        self.readers = {}
        self.last_dmat = None
        self.last_real = {e: None for e in ENGINES}
        self.tot = {"ld": 0, "st": 0}
        self.cur_ep = {"ld": 0, "st": 0}
        self.unsealed = {"ld": [], "st": []}
        self.ep_of = {}
        self.n_coll = 0

    @staticmethod
    def _key(ap):
        return ap.tensor.name

    def _seal(self, kl):
        if not self.unsealed[kl]:
            return
        self.cur_ep[kl] += 1
        for o in self.unsealed[kl]:
            self.ep_of[o] = (kl, self.cur_ep[kl])
        self.unsealed[kl] = []
        opid = len(self.ops)
        self.ops.append(dict(eng="sp", fn="EPOCH", kl=kl,
                             waits={("raw", kl, self.tot[kl])}, isdma=False))
        self.eng_ops["sp"].append(opid)

    def seal_all(self):
        self._seal("ld")
        self._seal("st")

    def op(self, eng, fn, reads=(), writes=(), dmat=False, after=(),
           selfsync=False, coll=False):
        waits = set()

        def dep(d):
            if d is None:
                return
            drec = self.ops[d]
            de = drec["eng"]
            if drec.get("coll"):
                waits.add(("cc", drec["coll_tot"]))
                return
            if drec["isdma"]:
                if eng == "sp":
                    waits.add(("raw", drec["kl"], drec["tot"]))
                else:
                    if d not in self.ep_of:
                        self._seal(drec["kl"])
                    waits.add(("ep",) + self.ep_of[d])
                return
            if de == eng and not selfsync:
                return
            waits.add(d)

        for a in reads:
            dep(self.writer.get(self._key(a)))
        for a in writes:
            dep(self.writer.get(self._key(a)))
            for r in self.readers.get(self._key(a), ()):
                dep(r)
        for d in after:
            dep(d)
        isdma = eng == "sp" and fn is not None
        kl = None
        if isdma:
            import concourse.bass as _bass
            spc = str(writes[0].space) if writes else "?"
            kl = "ld" if "SBUF" in spc or "PSUM" in spc else "st"
        if dmat:
            waits.add(("raw", "ld", self.tot["ld"]))
            waits.add(("raw", "st", self.tot["st"]))
        elif isdma and self.last_dmat is not None:
            waits.add(("raw", "ld", self.ops[self.last_dmat]["tot"]))
        opid = len(self.ops)
        rec = dict(eng=eng, fn=fn, waits=waits, isdma=isdma, kl=kl)
        if coll:
            self.n_coll += 1
            rec["coll"] = True
            rec["coll_tot"] = 16 * self.n_coll
        if isdma:
            self.tot[kl] += 16
            rec["tot"] = self.tot[kl]
            self.unsealed[kl].append(opid)
        if dmat:
            self.last_dmat = opid
        self.ops.append(rec)
        self.eng_ops[eng].append(opid)
        if fn is not None:
            self.last_real[eng] = opid
        for a in reads:
            self.readers.setdefault(self._key(a), []).append(opid)
        for a in writes:
            self.writer[self._key(a)] = opid
            self.readers[self._key(a)] = []
        return opid

    def barrier(self):
        self.seal_all()
        lasts = [self.last_real[e] for e in ENGINES if self.last_real[e] is not None]
        eps = dict(self.cur_ep)
        for e in ENGINES:
            oid = self.op(e, None, after=lasts)
            for kl in ("ld", "st"):
                if eps[kl]:
                    self.ops[oid]["waits"].add(("ep", kl, eps[kl]))

    def emit_all(self, block, sems):
        self.seal_all()
        marked = set()
        for rec in self.ops:
            for d in rec["waits"]:
                if isinstance(d, int):
                    marked.add(d)
        for oid in self.eng_ops["sp"]:
            if self.ops[oid]["isdma"]:
                marked.add(oid)
        val = {}
        for e in ENGINES:
            c = 0
            for oid in self.eng_ops[e]:
                if self.ops[oid].get("coll"):
                    continue
                if oid in marked:
                    assert self.ops[oid]["fn"] is not None, "wait on a no-op"
                    c += 16 if self.ops[oid]["isdma"] else 1
                    val[oid] = c
        streams = {
            "sp": block.sync, "pe": block.tensor, "act": block.scalar,
            "dve": block.vector, "pool": block.gpsimd,
        }
        semnames = list(ENGINES) + ["epl", "eps", "cc", "ld", "st"]
        epname = {"ld": "epl", "st": "eps"}

        def make(eng_name):
            oplist = self.eng_ops[eng_name]

            def emitter(eng_obj):
                waited = {s2: 0 for s2 in semnames}
                for oid in oplist:
                    rec = self.ops[oid]
                    need = {}
                    for d in rec["waits"]:
                        if isinstance(d, int):
                            de = self.ops[d]["eng"]
                            need[de] = max(need.get(de, 0), val[d])
                        elif d[0] == "ep":
                            nm = epname[d[1]]
                            need[nm] = max(need.get(nm, 0), d[2])
                        elif d[0] == "cc":
                            need["cc"] = max(need.get("cc", 0), d[1])
                        else:
                            assert eng_name == "sp"
                            nm = d[1]
                            need[nm] = max(need.get(nm, 0), d[2])
                    for se in semnames:
                        v = need.get(se, 0)
                        if v > waited[se]:
                            eng_obj.wait_ge(sems[se], v)
                            waited[se] = v
                    if rec["fn"] is None:
                        continue
                    if rec["fn"] == "EPOCH":
                        eng_obj.sem_inc(sems[epname[rec["kl"]]], 1)
                        continue
                    ins = rec["fn"](eng_obj)
                    if rec.get("coll"):
                        ins.then_inc(sems["cc"], 16)
                    elif oid in marked:
                        if rec["isdma"]:
                            ins.then_inc(sems[rec["kl"]], 16)
                        else:
                            ins.then_inc(sems[eng_name], 1)
            return emitter

        for e in ENGINES:
            if self.eng_ops[e]:
                streams[e](make(e))


def build_nc(stop_after=3, debug=False):
    nc = bass.Bass("TRN2", target_bir_lowering=False)

    Xd = nc.dram_tensor("x", [T, C], F32, kind="ExternalInput").ap()
    WT = {w: nc.dram_tensor(f"{w}T", [C, C], F32, kind="ExternalInput").ap()
          for w in ("wg", "wr", "wk", "wv", "wo")}
    Bv = {w: nc.dram_tensor(f"b{w}", [128, 8], F32, kind="ExternalInput").ap()
          for w in ("g", "r", "k", "v")}
    BOr = nc.dram_tensor("bo_row", [1, C], F32, kind="ExternalInput").ap()
    Mv = {w: nc.dram_tensor(f"m{w}", [128, 8], F32, kind="ExternalInput").ap()
          for w in ("g", "r", "k", "v")}
    Wvec = nc.dram_tensor("w_vec", [128, 8], F32, kind="ExternalInput").ap()
    Uvec = nc.dram_tensor("u_vec", [128, 8], F32, kind="ExternalInput").ap()
    Zc = nc.dram_tensor("zeros_c", [128, 8], F32, kind="ExternalInput").ap()
    Ones1 = nc.dram_tensor("ones1", [1, 128], BF16, kind="ExternalInput").ap()
    Ident = nc.dram_tensor("ident", [128, 128], BF16, kind="ExternalInput").ap()
    GSel = nc.dram_tensor("gnsel", [128, 8 * 16], BF16, kind="ExternalInput").ap()
    GBSel = nc.dram_tensor("gnbsel", [16, 8 * 128], BF16, kind="ExternalInput").ap()
    WGd = nc.dram_tensor("w_gate", [II, C], F32, kind="ExternalInput").ap()
    WUd = nc.dram_tensor("w_up", [II, C], F32, kind="ExternalInput").ap()
    WDd = nc.dram_tensor("w_down", [C, II], F32, kind="ExternalInput").ap()

    OUT = nc.dram_tensor("out", [T, C], F32, kind="ExternalOutput").ap()
    if debug:
        DBG = {
            "t_hcm": nc.dram_tensor("t_hcm", [128, 8 * 544], BF16, kind="ExternalOutput").ap(),
            "t_zg": nc.dram_tensor("t_zg", [128, 8 * 512], BF16, kind="ExternalOutput").ap(),
            "t_sg": nc.dram_tensor("t_sg", [128, 8 * 512], BF16, kind="ExternalOutput").ap(),
            "t_rw": nc.dram_tensor("t_rw", [128, 8 * 512], F32, kind="ExternalOutput").ap(),
            "t_inner": nc.dram_tensor("t_inner", [128, 8 * 512], BF16, kind="ExternalOutput").ap(),
            "t_gmean": nc.dram_tensor("t_gmean", [16, 512], F32, kind="ExternalOutput").ap(),
            "t_grstd": nc.dram_tensor("t_grstd", [16, 512], F32, kind="ExternalOutput").ap(),
        }
    WGall = nc.dram_tensor("wg_all", [128, 8, IT, 128], BF16, kind="Internal").ap()
    WUall = nc.dram_tensor("wu_all", [128, 8, IT, 128], BF16, kind="Internal").ap()
    WDall = nc.dram_tensor("wd_all", [128, IT, 8, 128], BF16, kind="Internal").ap()
    HQd = nc.dram_tensor("hq_d", [T, II], BF16, kind="Internal").ap()

    X_t = Xd.rearrange("(g p) c -> p g c", p=128)
    OUT_t = OUT.rearrange("(g p) c -> p g c", p=128)
    HQ_t = HQd.rearrange("(g p) i -> p g i", p=128)

    prog = Prog()

    with ExitStack() as outer:
        def osb(name, shape, dtype=F32):
            return outer.enter_context(nc.sbuf_tensor(name, shape, dtype))
        sems = {e: outer.enter_context(nc.semaphore(f"s_{e}")) for e in ENGINES}
        for nm in ("epl", "eps", "cc", "ld", "st"):
            sems[nm] = outer.enter_context(nc.semaphore(f"s_{nm}"))

        m_c = {w: osb(f"m_{w}", [128, 8]) for w in ("g", "r", "k", "v")}
        b_c = {w: osb(f"b_{w}", [128, 8]) for w in ("g", "r", "k", "v")}
        u_c = osb("u_c", [128, 8])
        w_c = osb("w_c", [128, 8])
        decay_c = osb("decay_c", [128, 8])
        carry = osb("carry", [128, 8])
        bo_f = osb("bo_f", [1, C])
        bo_bf = osb("bo_bf", [1, C], BF16)
        ones1 = osb("ones1_s", [1, 128], BF16)
        ident = osb("ident_s", [128, 128], BF16)
        gnsel = osb("gnsel_s", [128, 8, 16], BF16)
        gnbsel = osb("gnbsel_s", [16, 8, 128], BF16)

        def dma(dst, src, **kw):
            return prog.op("sp", lambda s: s.dma_start(out=dst, in_=src, **kw),
                           reads=[src], writes=[dst])

        def dmat(dst, src):
            return prog.op("sp", lambda s: s.dma_start_transpose(dst, src),
                           reads=[src], writes=[dst], dmat=True)

        def dmac(dst, src, **kw):
            return prog.op("pool", lambda s: s.dma_start(out=dst, in_=src, **kw),
                           reads=[src], writes=[dst], coll=True)

        def quake_rsqrt(eng, v, y, t1, vfull, yfull, t1full):
            prog.op(eng, lambda s: s.tensor_copy(t1, v.bitcast(I32)),
                    reads=[vfull], writes=[t1full], selfsync=True)
            prog.op(eng, lambda s: s.tensor_scalar(t1, t1, -0.5, MAGIC, ALU.mult, ALU.add),
                    reads=[t1full], writes=[t1full], selfsync=True)
            prog.op(eng, lambda s: s.tensor_copy(y.bitcast(I32), t1),
                    reads=[t1full], writes=[yfull], selfsync=True)
            for _ in range(2):
                prog.op(eng, lambda s: s.tensor_tensor(t1, y, y, ALU.mult),
                        reads=[yfull], writes=[t1full], selfsync=True)
                prog.op(eng, lambda s: s.tensor_tensor(t1, t1, v, ALU.mult),
                        reads=[t1full, vfull], writes=[t1full], selfsync=True)
                prog.op(eng, lambda s: s.tensor_scalar(t1, t1, -0.5, 1.5, ALU.mult, ALU.add),
                        reads=[t1full], writes=[t1full], selfsync=True)
                prog.op(eng, lambda s: s.tensor_tensor(y, y, t1, ALU.mult),
                        reads=[yfull, t1full], writes=[yfull], selfsync=True)

        with ExitStack() as pA:
            def sbA(name, shape, dtype=F32):
                return pA.enter_context(nc.sbuf_tensor(name, shape, dtype))
            wT_b = {w: sbA(f"{w}T_b", [128, 8, C], BF16)
                    for w in ("wg", "wr", "wk", "wv", "wo")}

            with ExitStack() as p0:
                def sb0(name, shape, dtype=F32):
                    return p0.enter_context(nc.sbuf_tensor(name, shape, dtype))

                for w in ("g", "r", "k", "v"):
                    dma(m_c[w][:], Mv[w])
                    dma(b_c[w][:], Bv[w])
                dma(u_c[:], Uvec)
                dma(w_c[:], Wvec)
                dma(carry[:], Zc)
                dma(bo_f[:], BOr)
                dma(ones1[:], Ones1)
                dma(ident[:], Ident)
                dma(gnsel[:], GSel.rearrange("p (a b) -> p a b", b=16))
                dma(gnbsel[:], GBSel.rearrange("p (a b) -> p a b", b=128))

                prog.op("act", lambda s: s.preload_act_table(AF.Exp))
                prog.op("act", lambda s: s.activation(decay_c[:], w_c[:], AF.Exp),
                        reads=[w_c[:]], writes=[decay_c[:]])
                prog.op("act", lambda s: s.activation(decay_c[:], decay_c[:], AF.Exp, scale=-1.0),
                        reads=[decay_c[:]], writes=[decay_c[:]])
                prog.op("dve", lambda s: s.tensor_copy(bo_bf[:], bo_f[:]),
                        reads=[bo_f[:]], writes=[bo_bf[:]])

                wstage = [sb0(f"wstg{i}", [128, II], F32) for i in range(3)]
                awstg = [sb0(f"awstg{i}", [128, C], F32) for i in range(2)]
                aw_jobs = [(w, ct) for w in ("wg", "wr", "wk", "wv", "wo")
                           for ct in range(CT)]
                aw_dpos = [0]
                aw_apos = [0]

                def emit_aw_dma(nj):
                    for _ in range(nj):
                        k = aw_dpos[0]
                        if k >= len(aw_jobs):
                            return
                        w, ct = aw_jobs[k]
                        dma(awstg[k % 2][:], WT[w][128 * ct:128 * (ct + 1), :])
                        aw_dpos[0] += 1

                def emit_aw_act(nj, limit):
                    for _ in range(nj):
                        k = aw_apos[0]
                        if k >= limit:
                            return
                        w, ct = aw_jobs[k]
                        stg = awstg[k % 2]
                        prog.op("act", (lambda s, d=wT_b[w][:, ct, :], x=stg[:]:
                                        s.activation(d, x, AF.Identity)),
                                reads=[stg[:]], writes=[wT_b[w][:]])
                        aw_apos[0] += 1

                qbA = [sb0(f"qbA{i}", [128, II], F16) for i in range(2)]
                qbR = [sb0(f"qbR{i}", [128, II], F16) for i in range(2)]
                qbM = sb0("qbM", [128, II], F16)
                qbH = sb0("qbH", [128, II], F16)
                qscp = [sb0(f"qsc{i}", [128, II // 16], F32) for i in range(2)]
                qrsp = [sb0(f"qrs{i}", [128, II // 16], F32) for i in range(2)]
                psT = [p0.enter_context(nc.psum_tensor(f"psT{i}", [128, 128], BF16))
                       for i in range(2)]

                def quant_stage1(src_ap, src_full, width, scbuf, rsbuf):
                    nb = width // 16
                    sc = scbuf[:, :nb]
                    rs = rsbuf[:, :nb]
                    src3 = src_ap.rearrange("p (b k) -> p b k", k=16)
                    prog.op("dve", (lambda s, o=sc, x=src3: s.tensor_reduce(
                        o, x, AX.X, ALU.max, apply_absolute_value=True)),
                        reads=[src_full], writes=[scbuf[:]])
                    prog.op("dve", (lambda s, o=sc: s.tensor_scalar(
                        o, o, 1.0 / 6.0, 1e-12, ALU.mult, ALU.max)),
                        reads=[scbuf[:]], writes=[scbuf[:]], selfsync=True)
                    prog.op("dve", (lambda s, o=rs, i_=sc: s.reciprocal(o, i_)),
                            reads=[scbuf[:]], writes=[rsbuf[:]], selfsync=True)

                def quant_stage2(src_ap, src_full, width, bi, scbuf, rsbuf):
                    bAt = qbA[bi]
                    bRt = qbR[bi]
                    nb = width // 16
                    sc = scbuf[:, :nb]
                    rs = rsbuf[:, :nb]
                    src3 = src_ap.rearrange("p (b k) -> p b k", k=16)
                    xn = bAt[:, :width]
                    rsb = rs.unsqueeze(2).broadcast_to([128, nb, 16])
                    scb = sc.unsqueeze(2).broadcast_to([128, nb, 16])
                    xn3 = xn.rearrange("p (b k) -> p b k", k=16)
                    prog.op("pool", (lambda s, o=xn3, x=src3, r=rsb:
                                     s.tensor_tensor(o, x, r, ALU.mult)),
                            reads=[src_full, rsbuf[:]], writes=[bAt[:]])
                    t = xn.bitcast(U16)
                    br = bRt[:, :width].bitcast(U16)
                    bm = qbM[:, :width].bitcast(U16)
                    bh = qbH[:, :width].bitcast(U16)
                    prog.op("dve", (lambda s: s.tensor_single_scalar(bm, t, 0x7FFF, ALU.bitwise_and)),
                            reads=[bAt[:]], writes=[qbM[:]])
                    prog.op("dve", (lambda s: s.tensor_single_scalar(br, t, 0x0100, ALU.add)),
                            reads=[bAt[:]], writes=[bRt[:]])
                    prog.op("dve", (lambda s: s.tensor_single_scalar(br, br, 0x7E00, ALU.bitwise_and)),
                            reads=[bRt[:]], writes=[bRt[:]], selfsync=True)
                    prog.op("dve", (lambda s: s.tensor_scalar(br, br, 0x3C00, 0x3800, ALU.max, ALU.subtract)),
                            reads=[bRt[:]], writes=[bRt[:]], selfsync=True)
                    prog.op("dve", (lambda s: s.tensor_scalar(bh, bm, 0x3400, 0x3800, ALU.is_gt, ALU.mult)),
                            reads=[qbM[:]], writes=[qbH[:]])
                    prog.op("dve", (lambda s: s.scalar_tensor_tensor(br, bm, 0x3A00, br, ALU.is_gt, ALU.mult)),
                            reads=[qbM[:], bRt[:]], writes=[bRt[:]])
                    prog.op("dve", (lambda s: s.tensor_tensor(br, br, bh, ALU.add)),
                            reads=[bRt[:], qbH[:]], writes=[bRt[:]], selfsync=True)
                    prog.op("dve", (lambda s: s.tensor_single_scalar(bm, t, 0x8000, ALU.bitwise_and)),
                            reads=[bAt[:]], writes=[qbM[:]])
                    prog.op("dve", (lambda s: s.tensor_tensor(br, bm, br, ALU.bitwise_or)),
                            reads=[qbM[:], bRt[:]], writes=[bRt[:]])
                    q3 = bRt[:, :width].rearrange("p (b k) -> p b k", k=16)
                    ob3 = bRt[:, :width].bitcast(BF16).rearrange("p (b k) -> p b k", k=16)
                    prog.op("pool", (lambda s, o=ob3, q=q3, ss=scb:
                                     s.tensor_tensor(o, q, ss, ALU.mult)),
                            reads=[bRt[:], scbuf[:]], writes=[bRt[:]])

                slabs = ([(WGd, "gu", WGall, [2 * i, 2 * i + 1], 2048) for i in range(11)]
                         + [(WUd, "gu", WUall, [2 * i, 2 * i + 1], 2048) for i in range(11)]
                         + [(WDd, "d", WDall, [k], II) for k in range(CT)])
                tstg = [sb0(f"tstg{i}", [128, II], BF16) for i in range(2)]
                NS = len(slabs)

                def slab_load(si):
                    srcd, kind, dstd, ks, width = slabs[si]
                    stg = wstage[si % 3]
                    if kind == "gu":
                        for ii, k in enumerate(ks):
                            dma(stg[:, C * ii:C * (ii + 1)],
                                srcd[128 * k:128 * (k + 1), :])
                    else:
                        dma(stg[:, :width], srcd[128 * ks[0]:128 * (ks[0] + 1), :])

                pend_store = []

                def flush_store():
                    while pend_store:
                        kind2, dstd2, ks2, width2, ts22 = pend_store.pop(0)
                        if kind2 == "gu":
                            for ii2, k2 in enumerate(ks2):
                                dmac(dstd2[:, :, k2, :],
                                     ts22[:, C * ii2:C * (ii2 + 1)].rearrange(
                                         "p (j r) -> p j r", r=128))
                        else:
                            dmac(dstd2[:, :, ks2[0], :],
                                 ts22[:, :width2].rearrange("p (m r) -> p m r", r=128))

                for si in range(2):
                    slab_load(si)
                emit_aw_dma(2)
                for si, (srcd, kind, dstd, ks, width) in enumerate(slabs):
                    stg = wstage[si % 3]
                    ts2 = tstg[si % 2]
                    flush_store()
                    quant_stage1(stg[:, :width], stg[:], width,
                                 qscp[si % 2], qrsp[si % 2])
                    emit_aw_act(2, aw_dpos[0])
                    if si + 2 < NS:
                        slab_load(si + 2)
                    emit_aw_dma(2)
                    quant_stage2(stg[:, :width], stg[:], width,
                                 si % 2, qscp[si % 2], qrsp[si % 2])
                    obt = qbR[si % 2]
                    for m in range(width // 128):
                        ps = psT[m % 2]
                        obv = obt[:, 128 * m:128 * (m + 1)].bitcast(BF16)
                        prog.op("pe", (lambda s, i_=obv, ps_=ps[:]:
                                       s.transpose(ps_, i_, ident[:])),
                                reads=[obt[:], ident[:]], writes=[ps[:]])
                        prog.op("act", (lambda s, o=ts2[:, 128 * m:128 * (m + 1)],
                                        ps_=ps[:]:
                                        s.activation(o, ps_, AF.Identity)),
                                reads=[ps[:]], writes=[ts2[:]])
                    pend_store.append((kind, dstd, ks, width, ts2))
                flush_store()
                while aw_apos[0] < len(aw_jobs):
                    emit_aw_dma(4)
                    emit_aw_act(4, aw_dpos[0])

                prog.barrier()

            x_bufs = [sbA(f"x_buf{i}", [128, 4, C], F32) for i in range(2)]
            h_toks = [sbA(f"h_tok{i}", [128, C], BF16) for i in range(2)]
            h_cm = sbA("h_cm", [128, 8, 544], BF16)
            d_t = sbA("d_t", [128, CHT], BF16)
            z_b = {w: sbA(f"z_{w}", [128, 8, CHT], BF16) for w in ("g", "r", "k", "v")}
            sg_full = sbA("sg_full", [128, 8, CHT], BF16)
            rw_full = sbA("rw_full", [128, 8, CHT], BF16)
            inner = sbA("inner", [128, 8, CHT], BF16)
            v_t = sbA("v_t", [128, CHT], F16)
            r_t = sbA("r_t", [128, CHT], BF16)
            kv_t = sbA("kv_t", [128, CHT], F32)
            wkv_t = sbA("wkv_t", [128, CHT], F32)
            rw2_t = [sbA(f"rw2_t{i}", [128, CHT], BF16) for i in range(3)]
            lnsum = sbA("lnsum", [128, 8])
            lnssq = sbA("lnssq", [128, 8])
            lnmean = sbA("lnmean", [128, 8])
            lnvar = sbA("lnvar", [128, 8])
            lnt1 = sbA("lnt1", [128, 8])
            lnrstd = sbA("lnrstd", [128, 8])
            lnnb = sbA("lnnb", [128, 8])
            gmean = sbA("gmean", [16, CHT])
            gvar = sbA("gvar", [16, CHT])
            gt1 = sbA("gt1", [16, CHT])
            grstd = sbA("grstd", [16, CHT])
            gmean_bf = gvar_holder = None

            psP = {w: pA.enter_context(nc.psum_tensor(f"ps_{w}", [128, CHT], F32))
                   for w in ("g", "r", "k", "v")}
            psSum = pA.enter_context(nc.psum_tensor("psSum", [16, CHT], F32))
            psSsq = pA.enter_context(nc.psum_tensor("psSsq", [16, CHT], F32))
            psMu = pA.enter_context(nc.psum_tensor("psMu", [128, CHT], F32))
            psRs = pA.enter_context(nc.psum_tensor("psRs", [128, CHT], F32))

            prog.op("act", lambda s: s.preload_act_table(AF.Sigmoid))


            def ln_stats_tile(xb, x_ap, col, junk_ap):
                prog.op("dve", (lambda s, o=lnsum[:, col:col + 1], x=x_ap:
                                s.reduce_sum(o, x, AX.X)),
                        reads=[xb[:]], writes=[lnsum[:]])
                prog.op("act", (lambda s, o=junk_ap[:, :512], x=x_ap[:, :512],
                                a=lnssq[:, col:col + 1]:
                                s.activation(o, x, AF.Square, accum_out=a)),
                        reads=[xb[:]], writes=[junk_ap, lnssq[:]])
                prog.op("act", (lambda s, o=junk_ap[:, 512:], x=x_ap[:, 512:],
                                a=lnsum[:, 4 + col:5 + col]:
                                s.activation(o, x, AF.Square, accum_out=a)),
                        reads=[xb[:]], writes=[lnsum[:]])
                prog.op("dve", (lambda s, o=lnssq[:, col:col + 1],
                                b=lnsum[:, 4 + col:5 + col]:
                                s.tensor_tensor(o, o, b, ALU.add)),
                        reads=[lnssq[:], lnsum[:]], writes=[lnssq[:]],
                        selfsync=True)

            def ln_finish(ncols):
                sl = slice(0, ncols)
                ms = lnmean[:, sl]; vs = lnvar[:, sl]
                t1 = lnt1[:, sl]
                ys = lnrstd[:, sl]; nb_ = lnnb[:, sl]
                prog.op("dve", (lambda s: s.tensor_scalar(ms, lnsum[:, sl], 1.0 / C, None, ALU.mult)),
                        reads=[lnsum[:]], writes=[lnmean[:]], selfsync=True)
                prog.op("dve", (lambda s: s.tensor_tensor(t1, ms, ms, ALU.mult)),
                        reads=[lnmean[:]], writes=[lnt1[:]], selfsync=True)
                prog.op("dve", (lambda s: s.scalar_tensor_tensor(vs, lnssq[:, sl], 1.0 / C, t1, ALU.mult, ALU.subtract)),
                        reads=[lnssq[:], lnt1[:]], writes=[lnvar[:]], selfsync=True)
                prog.op("dve", (lambda s: s.tensor_scalar(vs, vs, EPS, None, ALU.add)),
                        reads=[lnvar[:]], writes=[lnvar[:]], selfsync=True)
                quake_rsqrt("dve", vs, ys, t1, lnvar[:], lnrstd[:], lnt1[:])
                prog.op("dve", (lambda s: s.scalar_tensor_tensor(nb_, ms, -1.0, ys, ALU.mult, ALU.mult)),
                        reads=[lnmean[:], lnrstd[:]], writes=[lnnb[:]], selfsync=True)

            def ln_apply(xb, x_ap, out_ap, col):
                prog.op("act", (lambda s, o=out_ap, x=x_ap,
                                b=lnnb[:, col:col + 1], sc=lnrstd[:, col:col + 1]:
                                s.activation(o, x, AF.Identity, bias=b, scale=sc)),
                        reads=[xb[:], lnnb[:], lnrstd[:]], writes=[out_ap])

            dma(x_bufs[0][:, 0, :], X_t[:, NT - 1, :])
            ln_stats_tile(x_bufs[0], x_bufs[0][:, 0, :], 0, h_toks[0][:])
            ln_finish(1)
            ln_apply(x_bufs[0], x_bufs[0][:, 0, :], h_toks[0][:], 0)
            hlast = h_toks[1][:].rearrange("p (a b) -> p a b", b=128)
            dmat(hlast, h_toks[0][:])
            prog.op("dve", lambda s: s.tensor_copy(h_cm[:, :, 15:16], hlast[:, :, 127:128]),
                    reads=[h_toks[1][:]], writes=[h_cm[:]])

            def p1_head(n):
                xb = x_bufs[n % 2]
                ht = h_toks[n % 2]
                for a in range(4):
                    dma(xb[:, a, :], X_t[:, 4 * n + a, :])
                    ln_stats_tile(xb, xb[:, a, :], a, ht[:])
                ln_finish(4)
                if n > 0:
                    prog.op("dve", lambda s: s.tensor_copy(h_cm[:, :, 15:16], h_cm[:, :, 527:528]),
                            reads=[h_cm[:]], writes=[h_cm[:]])
                for a in range(4):
                    ln_apply(xb, xb[:, a, :], ht[:], a)
                    off = 16 + 128 * a
                    dmat(h_cm[:, :, off:off + 128], ht[:])

            def p1_mix(n):
                for j in range(CT):
                    h_j = h_cm[:, j, 16:528]
                    hs_j = h_cm[:, j, 15:527]
                    prog.op("pool", (lambda s, hj=h_j, hsj=hs_j:
                                     s.tensor_tensor(d_t[:], hj, hsj, ALU.subtract)),
                            reads=[h_cm[:]], writes=[d_t[:]], selfsync=True)
                    for w in ("g", "r", "k", "v"):
                        mwb = m_c[w][:, j:j + 1].broadcast_to([128, CHT])
                        prog.op("pool", (lambda s, zj=z_b[w][:, j, :], mw=mwb:
                                         s.tensor_tensor(zj, d_t[:], mw, ALU.mult)),
                                reads=[d_t[:], m_c[w][:]], writes=[z_b[w][:]], selfsync=True)
                        prog.op("pool", (lambda s, zj=z_b[w][:, j, :], hsj=hs_j:
                                         s.tensor_tensor(zj, zj, hsj, ALU.add)),
                                reads=[h_cm[:]], writes=[z_b[w][:]], selfsync=True)

            def p1_proj(n):
                pend = []
                for j in range(CT):
                    for w, wn in (("v", "wv"), ("k", "wk"), ("g", "wg"), ("r", "wr")):
                        for k in range(CT):
                            prog.op("pe", (lambda s, ps=psP[w][:], lt=wT_b[wn][:, k, 128 * j:128 * (j + 1)],
                                           rh=z_b[w][:, k, :], st=(k == 0), sp=(k == CT - 1):
                                           s.matmul(ps, lt, rh, start=st, stop=sp)),
                                    reads=[wT_b[wn][:], z_b[w][:]],
                                    writes=[psP[w][:]] if k in (0, CT - 1) else [])
                    prog.op("act", (lambda s, ps=psP["v"][:], b=b_c["v"][:, j:j + 1]:
                                    s.activation(v_t[:], ps, AF.Identity, bias=b)),
                            reads=[psP["v"][:], b_c["v"][:]], writes=[v_t[:]])
                    prog.op("dve", (lambda s, ps=psP["k"][:], b=b_c["k"][:, j:j + 1]:
                                    s.scalar_tensor_tensor(kv_t[:], ps, b, v_t[:], ALU.add, ALU.mult)),
                            reads=[psP["k"][:], b_c["k"][:], v_t[:]], writes=[kv_t[:]])
                    prog.op("act", (lambda s, o=sg_full[:, j, :], ps=psP["g"][:], b=b_c["g"][:, j:j + 1]:
                                    s.activation(o, ps, AF.Sigmoid, bias=b)),
                            reads=[psP["g"][:], b_c["g"][:]], writes=[sg_full[:]])
                    prog.op("act", (lambda s, ps=psP["r"][:], b=b_c["r"][:, j:j + 1]:
                                    s.activation(r_t[:], ps, AF.Identity, bias=b)),
                            reads=[psP["r"][:], b_c["r"][:]], writes=[r_t[:]])
                    prog.op("dve", (lambda s, d0=decay_c[:, j:j + 1].broadcast_to([128, CHT]),
                                    init=carry[:, j:j + 1]:
                                    s.tensor_tensor_scan(wkv_t[:], d0, kv_t[:], init, ALU.mult, ALU.add)),
                            reads=[decay_c[:], kv_t[:], carry[:]], writes=[wkv_t[:]])
                    prog.op("dve", (lambda s, cc=carry[:, j:j + 1]:
                                    s.tensor_copy(cc, wkv_t[:, CHT - 1:CHT])),
                            reads=[wkv_t[:]], writes=[carry[:]], selfsync=True)
                    prog.op("dve", (lambda s, u=u_c[:, j:j + 1]:
                                    s.scalar_tensor_tensor(wkv_t[:], kv_t[:], u, wkv_t[:], ALU.mult, ALU.add)),
                            reads=[kv_t[:], u_c[:], wkv_t[:]], writes=[wkv_t[:]])
                    prog.op("pool", (lambda s, o=rw_full[:, j, :]:
                                     s.tensor_tensor(o, r_t[:], wkv_t[:], ALU.mult)),
                            reads=[r_t[:], wkv_t[:]], writes=[rw_full[:]])
                    prog.op("pool", (lambda s, o=rw2_t[j % 3][:], i_=rw_full[:, j, :]:
                                     s.tensor_tensor(o, i_, i_, ALU.mult)),
                            reads=[rw_full[:]], writes=[rw2_t[j % 3][:]],
                            selfsync=True)
                    pend.append(j)
                    if len(pend) > 1:
                        emit_stat_mm(pend.pop(0))
                while pend:
                    emit_stat_mm(pend.pop(0))

            def emit_stat_mm(j):
                prog.op("pe", (lambda s, lt=gnsel[:, j, :], i_=rw_full[:, j, :],
                               st=(j == 0), sp=(j == CT - 1):
                               s.matmul(psSum[:], lt, i_, start=st, stop=sp)),
                        reads=[gnsel[:], rw_full[:]],
                        writes=[psSum[:]] if j in (0, CT - 1) else [])
                prog.op("pe", (lambda s, lt=gnsel[:, j, :], i_=rw2_t[j % 3][:],
                               st=(j == 0), sp=(j == CT - 1):
                               s.matmul(psSsq[:], lt, i_, start=st, stop=sp)),
                        reads=[gnsel[:], rw2_t[j % 3][:]],
                        writes=[psSsq[:]] if j in (0, CT - 1) else [])

            def p1_gn(n):
                prog.op("dve", lambda s: s.tensor_scalar(gmean[:], psSum[:], 1.0 / 64, None, ALU.mult),
                        reads=[psSum[:]], writes=[gmean[:]])
                prog.op("dve", lambda s: s.tensor_tensor(gt1[:], gmean[:], gmean[:], ALU.mult),
                        reads=[gmean[:]], writes=[gt1[:]])
                prog.op("dve", lambda s: s.scalar_tensor_tensor(gvar[:], psSsq[:], 1.0 / 64, gt1[:], ALU.mult, ALU.subtract),
                        reads=[psSsq[:], gt1[:]], writes=[gvar[:]])
                prog.op("dve", lambda s: s.tensor_scalar(gvar[:], gvar[:], EPS, None, ALU.add),
                        reads=[gvar[:]], writes=[gvar[:]])
                quake_rsqrt("dve", gvar[:], grstd[:], gt1[:],
                            gvar[:], grstd[:], gt1[:])
                gmean_bf = gvar[:].bitcast(BF16)[:, :CHT]
                grstd_bf = gt1[:].bitcast(BF16)[:, :CHT]
                prog.op("dve", lambda s, o=gmean_bf: s.tensor_copy(o, gmean[:]),
                        reads=[gmean[:]], writes=[gvar[:]], selfsync=True)
                prog.op("dve", lambda s, o=grstd_bf: s.tensor_copy(o, grstd[:]),
                        reads=[grstd[:]], writes=[gt1[:]], selfsync=True)
                for j in range(CT):
                    prog.op("pe", (lambda s, lt=gnbsel[:, j, :], rh=gmean_bf:
                                   s.matmul(psMu[:], lt, rh, start=True, stop=True)),
                            reads=[gnbsel[:], gvar[:]], writes=[psMu[:]])
                    prog.op("pe", (lambda s, lt=gnbsel[:, j, :], rh=grstd_bf:
                                   s.matmul(psRs[:], lt, rh, start=True, stop=True)),
                            reads=[gnbsel[:], gt1[:]], writes=[psRs[:]])
                    prog.op("dve", (lambda s, rw=rw_full[:, j, :]:
                                    s.tensor_tensor(kv_t[:], rw, psMu[:], ALU.subtract)),
                            reads=[rw_full[:], psMu[:]], writes=[kv_t[:]])
                    prog.op("dve", lambda s: s.tensor_tensor(wkv_t[:], kv_t[:], psRs[:], ALU.mult),
                            reads=[kv_t[:], psRs[:]], writes=[wkv_t[:]])
                    prog.op("pool", (lambda s, o=inner[:, j, :], sg=sg_full[:, j, :]:
                                     s.tensor_tensor(o, wkv_t[:], sg, ALU.mult)),
                            reads=[wkv_t[:], sg_full[:]], writes=[inner[:]])

            WO_ROT = ("g", "r", "k", "v")

            def p1_wo(n):
                xb = x_bufs[n % 2]

                def res1(i):
                    a, hh = divmod(i, 2)
                    ps = psP[WO_ROT[i % 4]]
                    prog.op("dve", (lambda s, ps_=ps[:], x=xb[:, a, 512 * hh:512 * (hh + 1)]:
                                    s.tensor_tensor(x, x, ps_, ALU.add)),
                            reads=[xb[:], ps[:]], writes=[xb[:]])
                    if hh == 1:
                        dma(OUT_t[:, 4 * n + a, :], xb[:, a, :])

                for i in range(8):
                    a, hh = divmod(i, 2)
                    ps = psP[WO_ROT[i % 4]]
                    for j in range(CT):
                        prog.op("pe", (lambda s, ps_=ps[:], lt=inner[:, j, 128 * a:128 * (a + 1)],
                                       rh=wT_b["wo"][:, j, 512 * hh:512 * (hh + 1)], st=(j == 0):
                                       s.matmul(ps_, lt, rh, start=st, stop=False)),
                                reads=[inner[:], wT_b["wo"][:]],
                                writes=[ps[:]] if j == 0 else [])
                    prog.op("pe", (lambda s, ps_=ps[:], rh=bo_bf[0:1, 512 * hh:512 * (hh + 1)]:
                                   s.matmul(ps_, ones1[:], rh, start=False, stop=True)),
                            reads=[ones1[:], bo_bf[:]], writes=[ps[:]])
                    if i >= 3:
                        res1(i - 3)
                for i in range(5, 8):
                    res1(i)

            p1_head(0)
            p1_mix(0)
            for n in range(NCH):
                p1_proj(n)
                if n + 1 < NCH:
                    p1_head(n + 1)
                p1_gn(n)
                if n + 1 < NCH:
                    p1_mix(n + 1)
                p1_wo(n)

            prog.barrier()

        if stop_after >= 2:
          with ExitStack() as pB:
            def sbB(name, shape, dtype=F32):
                return pB.enter_context(nc.sbuf_tensor(name, shape, dtype))
            wgT2 = [sbB(f"wgT2_{j}", [128, II], BF16) for j in range(CT)]
            wuT2 = [sbB(f"wuT2_{j}", [128, II], BF16) for j in range(CT)]
            x1_buf = sbB("x1_buf", [128, 4, C], F32)
            x1_buf2 = sbB("x1_buf2", [128, 4, C], F32)
            x2_t2 = [sbB("x2t2_0", [128, C], F32)]
            qb2p = {"x": tuple(sbB(f"qb2{nm}x", [128, C], F16) for nm in "AMH"),
                    "h": tuple(sbB(f"qb2{nm}h", [128, II], F16) for nm in "AMH")}
            qb2R = {"x": [sbB(f"qb2R{i}x", [128, C], F16) for i in range(2)],
                    "h": [sbB(f"qb2R{i}h", [128, II], F16) for i in range(2)]}
            qsc2p = {"x": sbB("qsc2_x", [128, C // 16]),
                     "h": sbB("qsc2_h", [128, II // 16])}
            qrs2p = {"x": sbB("qrs2_x", [128, C // 16]),
                     "h": sbB("qrs2_h", [128, II // 16])}
            xq_cm = [sbB(f"xq_cm{i}", [128, 8, 128], BF16) for i in range(2)]
            sil_t = [sbB(f"sil_t{i}", [128, 512], F16) for i in range(2)]
            up_t = [sbB(f"up_t{i}", [128, 512], F16) for i in range(2)]
            h16 = [sbB(f"h16_{i}", [128, II], F16) for i in range(3)]
            l2sum = sbB("l2sum", [128, 8])
            l2ssq = sbB("l2ssq", [128, 8])
            l2mean = sbB("l2mean", [128, 8])
            l2var = sbB("l2var", [128, 8])
            l2t1 = sbB("l2t1", [128, 8])
            l2rstd = sbB("l2rstd", [128, 8])
            l2nb = sbB("l2nb", [128, 8])
            l2ssqB = sbB("l2ssqB", [128, 8])
            l2nbs = sbB("l2nbs", [128, 8])
            l2rstds = sbB("l2rstds", [128, 8])
            psGt = [pB.enter_context(nc.psum_tensor(f"psGt{i}", [128, 512], F32))
                    for i in range(2)]
            psUp = [pB.enter_context(nc.psum_tensor(f"psUp{i}", [128, 512], F32))
                    for i in range(2)]

            prog.op("act", lambda s: s.preload_act_table(AF.Silu))


            def emit_quant2(src_ap, src_full, width, par, bi):
                bA, bM, bH = qb2p[par]
                bR = qb2R[par][bi]
                qsc2 = qsc2p[par]
                qrs2 = qrs2p[par]
                nb = width // 16
                sc = qsc2[:, :nb]; rs = qrs2[:, :nb]
                src3 = src_ap.rearrange("p (b k) -> p b k", k=16)
                prog.op("dve", (lambda s, o=sc, x=src3: s.tensor_reduce(
                    o, x, AX.X, ALU.max, apply_absolute_value=True)),
                    reads=[src_full], writes=[qsc2[:]])
                prog.op("dve", (lambda s, o=sc: s.tensor_scalar(
                    o, o, 1.0 / 6.0, 1e-12, ALU.mult, ALU.max)),
                    reads=[qsc2[:]], writes=[qsc2[:]], selfsync=True)
                prog.op("dve", (lambda s, o=rs, i_=sc: s.reciprocal(o, i_)),
                        reads=[qsc2[:]], writes=[qrs2[:]], selfsync=True)
                xn = bA[:, :width]
                rsb = rs.unsqueeze(2).broadcast_to([128, nb, 16])
                scb = sc.unsqueeze(2).broadcast_to([128, nb, 16])
                xn3 = xn.rearrange("p (b k) -> p b k", k=16)
                prog.op("pool", (lambda s, o=xn3, x=src3, r=rsb:
                                 s.tensor_tensor(o, x, r, ALU.mult)),
                        reads=[src_full, qrs2[:]], writes=[bA[:]])
                t = xn.bitcast(U16)
                br = bR[:, :width].bitcast(U16)
                bm = bM[:, :width].bitcast(U16)
                bh = bH[:, :width].bitcast(U16)
                prog.op("dve", (lambda s: s.tensor_single_scalar(bm, t, 0x7FFF, ALU.bitwise_and)),
                        reads=[bA[:]], writes=[bM[:]])
                prog.op("dve", (lambda s: s.tensor_single_scalar(br, t, 0x0100, ALU.add)),
                        reads=[bA[:]], writes=[bR[:]])
                prog.op("dve", (lambda s: s.tensor_single_scalar(br, br, 0x7E00, ALU.bitwise_and)),
                        reads=[bR[:]], writes=[bR[:]], selfsync=True)
                prog.op("dve", (lambda s: s.tensor_scalar(br, br, 0x3C00, 0x3800, ALU.max, ALU.subtract)),
                        reads=[bR[:]], writes=[bR[:]], selfsync=True)
                prog.op("dve", (lambda s: s.tensor_scalar(bh, bm, 0x3400, 0x3800, ALU.is_gt, ALU.mult)),
                        reads=[bM[:]], writes=[bH[:]])
                prog.op("dve", (lambda s: s.scalar_tensor_tensor(br, bm, 0x3A00, br, ALU.is_gt, ALU.mult)),
                        reads=[bM[:], bR[:]], writes=[bR[:]])
                prog.op("dve", (lambda s: s.tensor_tensor(br, br, bh, ALU.add)),
                        reads=[bR[:], bH[:]], writes=[bR[:]], selfsync=True)
                prog.op("dve", (lambda s: s.tensor_single_scalar(bm, t, 0x8000, ALU.bitwise_and)),
                        reads=[bA[:]], writes=[bM[:]])
                prog.op("dve", (lambda s: s.tensor_tensor(br, bm, br, ALU.bitwise_or)),
                        reads=[bM[:], bR[:]], writes=[bR[:]])
                q3 = bR[:, :width].rearrange("p (b k) -> p b k", k=16)
                ob3 = bR[:, :width].bitcast(BF16).rearrange("p (b k) -> p b k", k=16)
                prog.op("pool", (lambda s, o=ob3, q=q3, ss=scb:
                                 s.tensor_tensor(o, q, ss, ALU.mult)),
                        reads=[bR[:], qsc2[:]], writes=[bR[:]])

            WG4 = WGall.rearrange("p j k r -> p j (k r)")
            WU4 = WUall.rearrange("p j k r -> p j (k r)")

            IC = [(0, 512), (512, 512), (1024, 512), (1536, 512), (2048, 512), (2560, 256)]

            def ln2_finish(ncols):
                sl = slice(0, ncols)
                ms = l2mean[:, sl]; vs = l2var[:, sl]
                t1 = l2t1[:, sl]
                ys = l2rstd[:, sl]; nb_ = l2nb[:, sl]
                prog.op("dve", (lambda s: s.tensor_scalar(ms, l2sum[:, sl], 1.0 / C, None, ALU.mult)),
                        reads=[l2sum[:]], writes=[l2mean[:]], selfsync=True)
                prog.op("dve", (lambda s: s.tensor_tensor(t1, ms, ms, ALU.mult)),
                        reads=[l2mean[:]], writes=[l2t1[:]], selfsync=True)
                prog.op("dve", (lambda s: s.scalar_tensor_tensor(vs, l2ssq[:, sl], 1.0 / C, t1, ALU.mult, ALU.subtract)),
                        reads=[l2ssq[:], l2t1[:]], writes=[l2var[:]], selfsync=True)
                prog.op("dve", (lambda s: s.tensor_scalar(vs, vs, EPS, None, ALU.add)),
                        reads=[l2var[:]], writes=[l2var[:]], selfsync=True)
                quake_rsqrt("dve", vs, ys, t1, l2var[:], l2rstd[:], l2t1[:])
                prog.op("dve", (lambda s: s.scalar_tensor_tensor(nb_, ms, -1.0, ys, ALU.mult, ALU.mult)),
                        reads=[l2mean[:], l2rstd[:]], writes=[l2nb[:]], selfsync=True)

            x1_bufs = [x1_buf, x1_buf2]

            def p2a_load(n):
                for a in range(4):
                    dma(x1_bufs[n % 2][:, a, :], OUT_t[:, 4 * n + a, :])

            def p2a_prep(n):
                xb = x1_bufs[n % 2]
                for aa in range(4):
                    prog.op("dve", (lambda s, o=l2sum[:, aa:aa + 1], x=xb[:, aa, :]:
                                    s.reduce_sum(o, x, AX.X)),
                            reads=[xb[:]], writes=[l2sum[:]])
                    jk = x2_t2[0]
                    prog.op("act", (lambda s, x=xb[:, aa, :512], acc=l2ssq[:, aa:aa + 1],
                                    o=jk[:, :512]:
                                    s.activation(o, x, AF.Square, accum_out=acc)),
                            reads=[xb[:]], writes=[jk[:], l2ssq[:]])
                    prog.op("act", (lambda s, x=xb[:, aa, 512:], acc=l2ssqB[:, aa:aa + 1],
                                    o=jk[:, 512:]:
                                    s.activation(o, x, AF.Square, accum_out=acc)),
                            reads=[xb[:]], writes=[jk[:], l2ssqB[:]])
                    prog.op("dve", (lambda s, o=l2ssq[:, aa:aa + 1], b=l2ssqB[:, aa:aa + 1]:
                                    s.tensor_tensor(o, o, b, ALU.add)),
                            reads=[l2ssq[:], l2ssqB[:]], writes=[l2ssq[:]],
                            selfsync=True)
                ln2_finish(4)
                prog.op("dve", (lambda s, o=l2nbs[:, 4 * (n % 2):4 * (n % 2) + 4]:
                                s.tensor_copy(o, l2nb[:, 0:4])),
                        reads=[l2nb[:]], writes=[l2nbs[:]], selfsync=True)
                prog.op("dve", (lambda s, o=l2rstds[:, 4 * (n % 2):4 * (n % 2) + 4]:
                                s.tensor_copy(o, l2rstd[:, 0:4])),
                        reads=[l2rstd[:]], writes=[l2rstds[:]], selfsync=True)

            def p2a_front(g):
                n, a = divmod(g, 4)
                xb = x1_bufs[n % 2]
                pa = g % 2
                x2a = x2_t2[0]
                col = 4 * (n % 2) + a
                prog.op("act", (lambda s, o=x2a[:], x=xb[:, a, :],
                                b=l2nbs[:, col:col + 1], sc=l2rstds[:, col:col + 1]:
                                s.activation(o, x, AF.Identity, bias=b, scale=sc)),
                        reads=[xb[:], l2nbs[:], l2rstds[:]], writes=[x2a[:]])
                emit_quant2(x2a[:], x2a[:], C, "x", pa)
                xqv = qb2R["x"][pa][:, :C].bitcast(BF16)
                dmat(xq_cm[pa][:], xqv)

            def p2a_mid(g):
                pa = g % 2
                xca = xq_cm[pa]
                h16a = h16[g % 3]
                for ici, (io, iw) in enumerate(IC):
                    pg = psGt[ici % 2]
                    pu = psUp[ici % 2]
                    sl = sil_t[ici % 2]
                    for j in range(CT):
                        prog.op("pe", (lambda s, lt=xca[:, j, :], rh=wgT2[j][:, io:io + iw],
                                       ps=pg[:, :iw], st=(j == 0), sp=(j == CT - 1):
                                       s.matmul(ps, lt, rh, start=st, stop=sp)),
                                reads=[xca[:], wgT2[j][:]],
                                writes=[pg[:]] if j in (0, CT - 1) else [])
                    for j in range(CT):
                        prog.op("pe", (lambda s, lt=xca[:, j, :], rh=wuT2[j][:, io:io + iw],
                                       ps=pu[:, :iw], st=(j == 0), sp=(j == CT - 1):
                                       s.matmul(ps, lt, rh, start=st, stop=sp)),
                                reads=[xca[:], wuT2[j][:]],
                                writes=[pu[:]] if j in (0, CT - 1) else [])
                    prog.op("act", (lambda s, o=sl[:, :iw], ps=pg[:, :iw]:
                                    s.activation(o, ps, AF.Silu)),
                            reads=[pg[:]], writes=[sl[:]])
                    up = up_t[ici % 2]
                    prog.op("act", (lambda s, o=up[:, :iw], ps=pu[:, :iw]:
                                    s.activation(o, ps, AF.Identity)),
                            reads=[pu[:]], writes=[up[:]])
                    prog.op("pool", (lambda s, o=h16a[:, io:io + iw], si=sl[:, :iw], u=up[:, :iw]:
                                     s.tensor_tensor(o, si, u, ALU.mult)),
                            reads=[sl[:], up[:]], writes=[h16a[:]])

            hq_pend = []

            def p2a_back(g):
                while hq_pend:
                    g2 = hq_pend.pop(0)
                    dmac(HQ_t[:, g2, :], qb2R["h"][g2 % 2][:, :II].bitcast(BF16))
                emit_quant2(h16[g % 3][:], h16[g % 3][:], II, "h", g % 2)
                hq_pend.append(g)

            p2a_load(0)
            p2a_prep(0)
            for j in range(CT):
                dmac(wgT2[j][:], WG4[:, j, :])
                dmac(wuT2[j][:], WU4[:, j, :])
            p2a_front(0)
            for g in range(NT):
                if g % 4 == 1 and g // 4 + 1 < NCH:
                    p2a_load(g // 4 + 1)
                if g % 4 == 3 and g // 4 + 1 < NCH:
                    p2a_prep(g // 4 + 1)
                if g + 1 < NT:
                    p2a_front(g + 1)
                p2a_mid(g)
                if g >= 2:
                    p2a_back(g - 2)
            p2a_back(NT - 2)
            p2a_back(NT - 1)
            while hq_pend:
                g2 = hq_pend.pop(0)
                dmac(HQ_t[:, g2, :], qb2R["h"][g2 % 2][:, :II].bitcast(BF16))

            prog.barrier()

        if stop_after >= 3:
          with ExitStack() as pC:
            def sbC(name, shape, dtype=F32):
                return pC.enter_context(nc.sbuf_tensor(name, shape, dtype))
            wdT = sbC("wdT", [128, IT, C], BF16)
            hqT = [sbC(f"hqT{i}", [128, IT, 4, 128], BF16) for i in range(2)]
            x1b = [sbC(f"x1b{i}", [128, 4, C], F32) for i in range(2)]
            psD = [pC.enter_context(nc.psum_tensor(f"psD{i}", [128, 512], F32))
                   for i in range(8)]

            hstg = [sbC(f"hstg{i}", [128, II], BF16) for i in range(2)]
            dmac(wdT[:], WDall.rearrange("p m k r -> p m (k r)"))

            def p2b_prefetch(n):
                for a in range(4):
                    stg = hstg[a % 2]
                    dmac(stg[:], HQd[512 * n + 128 * a:512 * n + 128 * (a + 1), :])
                    dmat(hqT[n % 2][:, :, a, :], stg[:])
                for a in range(4):
                    dma(x1b[n % 2][:, a, :], OUT_t[:, 4 * n + a, :])

            p2b_prefetch(0)
            for n in range(NCH):
                hq_n = hqT[n % 2]
                x1_n = x1b[n % 2]
                for m in range(IT):
                    for a in range(4):
                        for hh in range(2):
                            prog.op("pe", (lambda s, ps=psD[a * 2 + hh][:],
                                           lt=hq_n[:, m, a, :],
                                           rh=wdT[:, m, 512 * hh:512 * (hh + 1)],
                                           st=(m == 0), sp=(m == IT - 1):
                                           s.matmul(ps, lt, rh, start=st, stop=sp)),
                                    reads=[hq_n[:], wdT[:]],
                                    writes=[psD[a * 2 + hh][:]] if m in (0, IT - 1) else [])
                if n + 1 < NCH:
                    p2b_prefetch(n + 1)
                for a in range(4):
                    for hh in range(2):
                        prog.op("dve", (lambda s, x=x1_n[:, a, 512 * hh:512 * (hh + 1)],
                                        ps=psD[a * 2 + hh][:]:
                                        s.tensor_tensor(x, x, ps, ALU.add)),
                                reads=[x1_n[:], psD[a * 2 + hh][:]], writes=[x1_n[:]])
                    dma(OUT_t[:, 4 * n + a, :], x1_n[:, a, :])

            prog.barrier()

        with nc.Block("main") as blk:
            prog.emit_all(blk, sems)

    return nc


_NC_CACHE = None


def kernel(**inputs):
    global _NC_CACHE
    f32 = np.float32
    bf16 = ml_dtypes.bfloat16
    x = np.ascontiguousarray(inputs["L0_attn_x"], dtype=f32)

    def colmaj(v):
        return np.ascontiguousarray(np.asarray(v, f32).reshape(8, 128).T)

    base = {}
    for nm, wk in (("wg", "Wg"), ("wr", "Wr"), ("wk", "Wk"), ("wv", "Wv"), ("wo", "Wo")):
        base[f"{nm}T"] = np.ascontiguousarray(
            np.asarray(inputs[f"L0_attn_{wk}_w"], f32).T)
    for nm, wk in (("g", "Wg"), ("r", "Wr"), ("k", "Wk"), ("v", "Wv")):
        base[f"b{nm}"] = colmaj(inputs[f"L0_attn_{wk}_b"])
    base["bo_row"] = np.ascontiguousarray(
        np.asarray(inputs["L0_attn_Wo_b"], f32).reshape(1, C))
    for nm in ("g", "r", "k", "v"):
        base[f"m{nm}"] = colmaj(inputs[f"L0_attn_m{nm}"])
    base["w_vec"] = colmaj(np.asarray(inputs["L0_attn_w"], f32).reshape(-1))
    base["u_vec"] = colmaj(np.asarray(inputs["L0_attn_u"], f32).reshape(-1))
    base["zeros_c"] = np.zeros((128, 8), f32)
    base["ones1"] = np.ones((1, 128), bf16)
    base["ident"] = np.eye(128, dtype=bf16)

    gnsel = np.zeros((128, 8, 16), bf16)
    gnbsel = np.zeros((16, 8, 128), bf16)
    for j in range(8):
        for p in range(128):
            hh = 2 * j + p // 64
            gnsel[p, j, hh] = 1
            gnbsel[hh, j, p] = 1
    base["gnsel"] = np.ascontiguousarray(gnsel.reshape(128, 8 * 16))
    base["gnbsel"] = np.ascontiguousarray(gnbsel.reshape(16, 8 * 128))
    base["w_gate"] = np.ascontiguousarray(inputs["L0_ffn_w_gate"], dtype=f32)
    base["w_up"] = np.ascontiguousarray(inputs["L0_ffn_w_up"], dtype=f32)
    base["w_down"] = np.ascontiguousarray(inputs["L0_ffn_w_down"], dtype=f32)

    if _NC_CACHE is None:
        _NC_CACHE = build_nc()
    nc = _NC_CACHE

    in_maps = []
    for b in range(8):
        m = dict(base)
        m["x"] = np.ascontiguousarray(x[b])
        in_maps.append(m)

    res = run_bass_kernel_spmd(nc, in_maps, core_ids=list(range(8)))
    out = np.stack([np.asarray(res.results[b]["out"]) for b in range(8)], axis=0)
    return out.astype(np.float32)


# version 38
# speedup vs baseline: 1.2858x; 1.2858x over previous
import numpy as np
import ml_dtypes
from contextlib import ExitStack

import concourse.bass as bass
import concourse.mybir as mybir
from concourse.bass_utils import run_bass_kernel_spmd

F32 = mybir.dt.float32
F16 = mybir.dt.float16
BF16 = mybir.dt.bfloat16
U16 = mybir.dt.uint16
I32 = mybir.dt.int32
ALU = mybir.AluOpType
AF = mybir.ActivationFunctionType
AX = mybir.AxisListType

T = 2048
C = 1024
II = 2816
NT = 16
NCH = 4
CT = 8
IT = 22
CHT = 512
EPS = 1e-5
MAGIC = 1597463007.0

ENGINES = ("sp", "pe", "act", "dve", "pool")


class Prog:

    def __init__(self):
        self.ops = []
        self.eng_ops = {e: [] for e in ENGINES}
        self.writer = {}
        self.readers = {}
        self.last_dmat = None
        self.last_real = {e: None for e in ENGINES}
        self.tot = {"ld": 0, "st": 0}
        self.cur_ep = {"ld": 0, "st": 0}
        self.unsealed = {"ld": [], "st": []}
        self.ep_of = {}
        self.n_coll = 0

    @staticmethod
    def _key(ap):
        return ap.tensor.name

    def _seal(self, kl):
        if not self.unsealed[kl]:
            return
        self.cur_ep[kl] += 1
        for o in self.unsealed[kl]:
            self.ep_of[o] = (kl, self.cur_ep[kl])
        self.unsealed[kl] = []
        opid = len(self.ops)
        self.ops.append(dict(eng="sp", fn="EPOCH", kl=kl,
                             waits={("raw", kl, self.tot[kl])}, isdma=False))
        self.eng_ops["sp"].append(opid)

    def seal_all(self):
        self._seal("ld")
        self._seal("st")

    def op(self, eng, fn, reads=(), writes=(), dmat=False, after=(),
           selfsync=False, coll=False):
        waits = set()

        def dep(d):
            if d is None:
                return
            drec = self.ops[d]
            de = drec["eng"]
            if drec.get("coll"):
                waits.add(("cc", drec["coll_tot"]))
                return
            if drec["isdma"]:
                if eng == "sp":
                    waits.add(("raw", drec["kl"], drec["tot"]))
                else:
                    if d not in self.ep_of:
                        self._seal(drec["kl"])
                    waits.add(("ep",) + self.ep_of[d])
                return
            if de == eng and not selfsync:
                return
            waits.add(d)

        for a in reads:
            dep(self.writer.get(self._key(a)))
        for a in writes:
            dep(self.writer.get(self._key(a)))
            for r in self.readers.get(self._key(a), ()):
                dep(r)
        for d in after:
            dep(d)
        isdma = eng == "sp" and fn is not None
        kl = None
        if isdma:
            import concourse.bass as _bass
            spc = str(writes[0].space) if writes else "?"
            kl = "ld" if "SBUF" in spc or "PSUM" in spc else "st"
        if dmat:
            waits.add(("raw", "ld", self.tot["ld"]))
            waits.add(("raw", "st", self.tot["st"]))
        elif isdma and self.last_dmat is not None:
            waits.add(("raw", "ld", self.ops[self.last_dmat]["tot"]))
        opid = len(self.ops)
        rec = dict(eng=eng, fn=fn, waits=waits, isdma=isdma, kl=kl)
        if coll:
            self.n_coll += 1
            rec["coll"] = True
            rec["coll_tot"] = 16 * self.n_coll
        if isdma:
            self.tot[kl] += 16
            rec["tot"] = self.tot[kl]
            self.unsealed[kl].append(opid)
        if dmat:
            self.last_dmat = opid
        self.ops.append(rec)
        self.eng_ops[eng].append(opid)
        if fn is not None:
            self.last_real[eng] = opid
        for a in reads:
            self.readers.setdefault(self._key(a), []).append(opid)
        for a in writes:
            self.writer[self._key(a)] = opid
            self.readers[self._key(a)] = []
        return opid

    def barrier(self):
        self.seal_all()
        lasts = [self.last_real[e] for e in ENGINES if self.last_real[e] is not None]
        eps = dict(self.cur_ep)
        for e in ENGINES:
            oid = self.op(e, None, after=lasts)
            for kl in ("ld", "st"):
                if eps[kl]:
                    self.ops[oid]["waits"].add(("ep", kl, eps[kl]))

    def emit_all(self, block, sems):
        self.seal_all()
        marked = set()
        for rec in self.ops:
            for d in rec["waits"]:
                if isinstance(d, int):
                    marked.add(d)
        for oid in self.eng_ops["sp"]:
            if self.ops[oid]["isdma"]:
                marked.add(oid)
        val = {}
        for e in ENGINES:
            c = 0
            for oid in self.eng_ops[e]:
                if self.ops[oid].get("coll"):
                    continue
                if oid in marked:
                    assert self.ops[oid]["fn"] is not None, "wait on a no-op"
                    c += 16 if self.ops[oid]["isdma"] else 1
                    val[oid] = c
        streams = {
            "sp": block.sync, "pe": block.tensor, "act": block.scalar,
            "dve": block.vector, "pool": block.gpsimd,
        }
        semnames = list(ENGINES) + ["epl", "eps", "cc", "ld", "st"]
        epname = {"ld": "epl", "st": "eps"}

        def make(eng_name):
            oplist = self.eng_ops[eng_name]

            def emitter(eng_obj):
                waited = {s2: 0 for s2 in semnames}
                for oid in oplist:
                    rec = self.ops[oid]
                    need = {}
                    for d in rec["waits"]:
                        if isinstance(d, int):
                            de = self.ops[d]["eng"]
                            need[de] = max(need.get(de, 0), val[d])
                        elif d[0] == "ep":
                            nm = epname[d[1]]
                            need[nm] = max(need.get(nm, 0), d[2])
                        elif d[0] == "cc":
                            need["cc"] = max(need.get("cc", 0), d[1])
                        else:
                            assert eng_name == "sp"
                            nm = d[1]
                            need[nm] = max(need.get(nm, 0), d[2])
                    for se in semnames:
                        v = need.get(se, 0)
                        if v > waited[se]:
                            eng_obj.wait_ge(sems[se], v)
                            waited[se] = v
                    if rec["fn"] is None:
                        continue
                    if rec["fn"] == "EPOCH":
                        eng_obj.sem_inc(sems[epname[rec["kl"]]], 1)
                        continue
                    ins = rec["fn"](eng_obj)
                    if rec.get("coll"):
                        ins.then_inc(sems["cc"], 16)
                    elif oid in marked:
                        if rec["isdma"]:
                            ins.then_inc(sems[rec["kl"]], 16)
                        else:
                            ins.then_inc(sems[eng_name], 1)
            return emitter

        for e in ENGINES:
            if self.eng_ops[e]:
                streams[e](make(e))


def build_nc(stop_after=3, debug=False):
    nc = bass.Bass("TRN2", target_bir_lowering=False)

    Xd = nc.dram_tensor("x", [T, C], F32, kind="ExternalInput").ap()
    WT = {w: nc.dram_tensor(f"{w}T", [C, C], F32, kind="ExternalInput").ap()
          for w in ("wg", "wr", "wk", "wv", "wo")}
    Bv = {w: nc.dram_tensor(f"b{w}", [128, 8], F32, kind="ExternalInput").ap()
          for w in ("g", "r", "k", "v")}
    BOr = nc.dram_tensor("bo_row", [1, C], F32, kind="ExternalInput").ap()
    Mv = {w: nc.dram_tensor(f"m{w}", [128, 8], F32, kind="ExternalInput").ap()
          for w in ("g", "r", "k", "v")}
    Wvec = nc.dram_tensor("w_vec", [128, 8], F32, kind="ExternalInput").ap()
    Uvec = nc.dram_tensor("u_vec", [128, 8], F32, kind="ExternalInput").ap()
    Zc = nc.dram_tensor("zeros_c", [128, 8], F32, kind="ExternalInput").ap()
    Ones1 = nc.dram_tensor("ones1", [1, 128], BF16, kind="ExternalInput").ap()
    Ident = nc.dram_tensor("ident", [128, 128], BF16, kind="ExternalInput").ap()
    GSel = nc.dram_tensor("gnsel", [128, 8 * 16], BF16, kind="ExternalInput").ap()
    GBSel = nc.dram_tensor("gnbsel", [16, 8 * 128], BF16, kind="ExternalInput").ap()
    WGd = nc.dram_tensor("w_gate", [II, C], F32, kind="ExternalInput").ap()
    WUd = nc.dram_tensor("w_up", [II, C], F32, kind="ExternalInput").ap()
    WDd = nc.dram_tensor("w_down", [C, II], F32, kind="ExternalInput").ap()

    OUT = nc.dram_tensor("out", [T, C], F32, kind="ExternalOutput").ap()
    if debug:
        DBG = {
            "t_hcm": nc.dram_tensor("t_hcm", [128, 8 * 544], BF16, kind="ExternalOutput").ap(),
            "t_zg": nc.dram_tensor("t_zg", [128, 8 * 512], BF16, kind="ExternalOutput").ap(),
            "t_sg": nc.dram_tensor("t_sg", [128, 8 * 512], BF16, kind="ExternalOutput").ap(),
            "t_rw": nc.dram_tensor("t_rw", [128, 8 * 512], F32, kind="ExternalOutput").ap(),
            "t_inner": nc.dram_tensor("t_inner", [128, 8 * 512], BF16, kind="ExternalOutput").ap(),
            "t_gmean": nc.dram_tensor("t_gmean", [16, 512], F32, kind="ExternalOutput").ap(),
            "t_grstd": nc.dram_tensor("t_grstd", [16, 512], F32, kind="ExternalOutput").ap(),
        }
    WGall = nc.dram_tensor("wg_all", [128, 8, IT, 128], BF16, kind="Internal").ap()
    WUall = nc.dram_tensor("wu_all", [128, 8, IT, 128], BF16, kind="Internal").ap()
    WDall = nc.dram_tensor("wd_all", [128, IT, 8, 128], BF16, kind="Internal").ap()
    HQd = nc.dram_tensor("hq_d", [T, II], BF16, kind="Internal").ap()

    X_t = Xd.rearrange("(g p) c -> p g c", p=128)
    OUT_t = OUT.rearrange("(g p) c -> p g c", p=128)
    HQ_t = HQd.rearrange("(g p) i -> p g i", p=128)

    prog = Prog()

    with ExitStack() as outer:
        def osb(name, shape, dtype=F32):
            return outer.enter_context(nc.sbuf_tensor(name, shape, dtype))
        sems = {e: outer.enter_context(nc.semaphore(f"s_{e}")) for e in ENGINES}
        for nm in ("epl", "eps", "cc", "ld", "st"):
            sems[nm] = outer.enter_context(nc.semaphore(f"s_{nm}"))

        m_c = {w: osb(f"m_{w}", [128, 8]) for w in ("g", "r", "k", "v")}
        b_c = {w: osb(f"b_{w}", [128, 8]) for w in ("g", "r", "k", "v")}
        u_c = osb("u_c", [128, 8])
        w_c = osb("w_c", [128, 8])
        decay_c = osb("decay_c", [128, 8])
        carry = osb("carry", [128, 8])
        bo_bf = osb("bo_bf", [1, C], BF16)
        ones1 = osb("ones1_s", [1, 128], BF16)
        ident = osb("ident_s", [128, 128], BF16)
        gnsel = osb("gnsel_s", [128, 8, 16], BF16)
        gnbsel = osb("gnbsel_s", [16, 8, 128], BF16)

        def dma(dst, src, **kw):
            return prog.op("sp", lambda s: s.dma_start(out=dst, in_=src, **kw),
                           reads=[src], writes=[dst])

        def dmat(dst, src):
            return prog.op("sp", lambda s: s.dma_start_transpose(dst, src),
                           reads=[src], writes=[dst], dmat=True)

        def dmac(dst, src, **kw):
            return prog.op("pool", lambda s: s.dma_start(out=dst, in_=src, **kw),
                           reads=[src], writes=[dst], coll=True)

        def quake_rsqrt(eng, v, y, t1, vfull, yfull, t1full):
            prog.op(eng, lambda s: s.tensor_copy(t1, v.bitcast(I32)),
                    reads=[vfull], writes=[t1full], selfsync=True)
            prog.op(eng, lambda s: s.tensor_scalar(t1, t1, -0.5, MAGIC, ALU.mult, ALU.add),
                    reads=[t1full], writes=[t1full], selfsync=True)
            prog.op(eng, lambda s: s.tensor_copy(y.bitcast(I32), t1),
                    reads=[t1full], writes=[yfull], selfsync=True)
            for _ in range(2):
                prog.op(eng, lambda s: s.tensor_tensor(t1, y, y, ALU.mult),
                        reads=[yfull], writes=[t1full], selfsync=True)
                prog.op(eng, lambda s: s.tensor_tensor(t1, t1, v, ALU.mult),
                        reads=[t1full, vfull], writes=[t1full], selfsync=True)
                prog.op(eng, lambda s: s.tensor_scalar(t1, t1, -0.5, 1.5, ALU.mult, ALU.add),
                        reads=[t1full], writes=[t1full], selfsync=True)
                prog.op(eng, lambda s: s.tensor_tensor(y, y, t1, ALU.mult),
                        reads=[yfull, t1full], writes=[yfull], selfsync=True)

        with ExitStack() as pA:
            def sbA(name, shape, dtype=F32):
                return pA.enter_context(nc.sbuf_tensor(name, shape, dtype))
            wT_b = {w: sbA(f"{w}T_b", [128, 8, C], BF16)
                    for w in ("wg", "wr", "wk", "wv", "wo")}

            with ExitStack() as p0:
                def sb0(name, shape, dtype=F32):
                    return p0.enter_context(nc.sbuf_tensor(name, shape, dtype))

                bo_f = sb0("bo_f", [1, C])
                for w in ("g", "r", "k", "v"):
                    dma(m_c[w][:], Mv[w])
                    dma(b_c[w][:], Bv[w])
                dma(u_c[:], Uvec)
                dma(w_c[:], Wvec)
                dma(carry[:], Zc)
                dma(bo_f[:], BOr)
                dma(ones1[:], Ones1)
                dma(ident[:], Ident)
                dma(gnsel[:], GSel.rearrange("p (a b) -> p a b", b=16))
                dma(gnbsel[:], GBSel.rearrange("p (a b) -> p a b", b=128))

                prog.op("act", lambda s: s.preload_act_table(AF.Exp))
                prog.op("act", lambda s: s.activation(decay_c[:], w_c[:], AF.Exp),
                        reads=[w_c[:]], writes=[decay_c[:]])
                prog.op("act", lambda s: s.activation(decay_c[:], decay_c[:], AF.Exp, scale=-1.0),
                        reads=[decay_c[:]], writes=[decay_c[:]])
                prog.op("dve", lambda s: s.tensor_copy(bo_bf[:], bo_f[:]),
                        reads=[bo_f[:]], writes=[bo_bf[:]])

                wstage = [sb0(f"wstg{i}", [128, II], F32) for i in range(3)]
                awstg = [sb0(f"awstg{i}", [128, C], F32) for i in range(2)]
                aw_jobs = [(w, ct) for w in ("wg", "wr", "wk", "wv", "wo")
                           for ct in range(CT)]
                aw_dpos = [0]
                aw_apos = [0]

                def emit_aw_dma(nj):
                    for _ in range(nj):
                        k = aw_dpos[0]
                        if k >= len(aw_jobs):
                            return
                        w, ct = aw_jobs[k]
                        dma(awstg[k % 2][:], WT[w][128 * ct:128 * (ct + 1), :])
                        aw_dpos[0] += 1

                def emit_aw_act(nj, limit):
                    for _ in range(nj):
                        k = aw_apos[0]
                        if k >= limit:
                            return
                        w, ct = aw_jobs[k]
                        stg = awstg[k % 2]
                        prog.op("act", (lambda s, d=wT_b[w][:, ct, :], x=stg[:]:
                                        s.activation(d, x, AF.Identity)),
                                reads=[stg[:]], writes=[wT_b[w][:]])
                        aw_apos[0] += 1

                qbA = [sb0(f"qbA{i}", [128, II], F16) for i in range(2)]
                qbR = [sb0(f"qbR{i}", [128, II], F16) for i in range(2)]
                qbM = sb0("qbM", [128, II], F16)
                qbH = sb0("qbH", [128, II], F16)
                qscp = [sb0(f"qsc{i}", [128, II // 16], F32) for i in range(2)]
                qrsp = [sb0(f"qrs{i}", [128, II // 16], F32) for i in range(2)]
                psT = [p0.enter_context(nc.psum_tensor(f"psT{i}", [128, 128], BF16))
                       for i in range(2)]

                def quant_stage1(src_ap, src_full, width, scbuf, rsbuf):
                    nb = width // 16
                    sc = scbuf[:, :nb]
                    rs = rsbuf[:, :nb]
                    src3 = src_ap.rearrange("p (b k) -> p b k", k=16)
                    prog.op("dve", (lambda s, o=sc, x=src3: s.tensor_reduce(
                        o, x, AX.X, ALU.max, apply_absolute_value=True)),
                        reads=[src_full], writes=[scbuf[:]])
                    prog.op("dve", (lambda s, o=sc: s.tensor_scalar(
                        o, o, 1.0 / 6.0, 1e-12, ALU.mult, ALU.max)),
                        reads=[scbuf[:]], writes=[scbuf[:]], selfsync=True)
                    prog.op("dve", (lambda s, o=rs, i_=sc: s.reciprocal(o, i_)),
                            reads=[scbuf[:]], writes=[rsbuf[:]], selfsync=True)

                def quant_stage2(src_ap, src_full, width, bi, scbuf, rsbuf):
                    bAt = qbA[bi]
                    bRt = qbR[bi]
                    nb = width // 16
                    sc = scbuf[:, :nb]
                    rs = rsbuf[:, :nb]
                    src3 = src_ap.rearrange("p (b k) -> p b k", k=16)
                    xn = bAt[:, :width]
                    rsb = rs.unsqueeze(2).broadcast_to([128, nb, 16])
                    scb = sc.unsqueeze(2).broadcast_to([128, nb, 16])
                    xn3 = xn.rearrange("p (b k) -> p b k", k=16)
                    prog.op("dve", (lambda s, o=xn3, x=src3, r=rsb:
                                    s.tensor_tensor(o, x, r, ALU.mult)),
                            reads=[src_full, rsbuf[:]], writes=[bAt[:]])
                    t = xn.bitcast(U16)
                    br = bRt[:, :width].bitcast(U16)
                    bm = qbM[:, :width].bitcast(U16)
                    bh = qbH[:, :width].bitcast(U16)
                    prog.op("dve", (lambda s: s.tensor_single_scalar(bm, t, 0x7FFF, ALU.bitwise_and)),
                            reads=[bAt[:]], writes=[qbM[:]])
                    prog.op("dve", (lambda s: s.tensor_single_scalar(br, t, 0x0100, ALU.add)),
                            reads=[bAt[:]], writes=[bRt[:]])
                    prog.op("dve", (lambda s: s.tensor_single_scalar(br, br, 0x7E00, ALU.bitwise_and)),
                            reads=[bRt[:]], writes=[bRt[:]], selfsync=True)
                    prog.op("dve", (lambda s: s.tensor_scalar(br, br, 0x3C00, 0x3800, ALU.max, ALU.subtract)),
                            reads=[bRt[:]], writes=[bRt[:]], selfsync=True)
                    prog.op("dve", (lambda s: s.tensor_scalar(bh, bm, 0x3400, 0x3800, ALU.is_gt, ALU.mult)),
                            reads=[qbM[:]], writes=[qbH[:]])
                    prog.op("dve", (lambda s: s.scalar_tensor_tensor(br, bm, 0x3A00, br, ALU.is_gt, ALU.mult)),
                            reads=[qbM[:], bRt[:]], writes=[bRt[:]])
                    prog.op("dve", (lambda s: s.tensor_tensor(br, br, bh, ALU.add)),
                            reads=[bRt[:], qbH[:]], writes=[bRt[:]], selfsync=True)
                    prog.op("dve", (lambda s: s.tensor_single_scalar(bm, t, 0x8000, ALU.bitwise_and)),
                            reads=[bAt[:]], writes=[qbM[:]])
                    prog.op("dve", (lambda s: s.tensor_tensor(br, bm, br, ALU.bitwise_or)),
                            reads=[qbM[:], bRt[:]], writes=[bRt[:]])
                    q3 = bRt[:, :width].rearrange("p (b k) -> p b k", k=16)
                    ob3 = bRt[:, :width].bitcast(BF16).rearrange("p (b k) -> p b k", k=16)
                    prog.op("pool", (lambda s, o=ob3, q=q3, ss=scb:
                                     s.tensor_tensor(o, q, ss, ALU.mult)),
                            reads=[bRt[:], scbuf[:]], writes=[bRt[:]])

                slabs = ([(WGd, "gu", WGall, [2 * i, 2 * i + 1], 2048) for i in range(11)]
                         + [(WUd, "gu", WUall, [2 * i, 2 * i + 1], 2048) for i in range(11)]
                         + [(WDd, "d", WDall, [k], II) for k in range(CT)])
                tstg = [sb0(f"tstg{i}", [128, II], BF16) for i in range(2)]
                NS = len(slabs)

                def slab_load(si):
                    srcd, kind, dstd, ks, width = slabs[si]
                    stg = wstage[si % 3]
                    if kind == "gu":
                        for ii, k in enumerate(ks):
                            dma(stg[:, C * ii:C * (ii + 1)],
                                srcd[128 * k:128 * (k + 1), :])
                    else:
                        dma(stg[:, :width], srcd[128 * ks[0]:128 * (ks[0] + 1), :])

                pend_store = []

                def flush_store():
                    while pend_store:
                        kind2, dstd2, ks2, width2, ts22 = pend_store.pop(0)
                        if kind2 == "gu":
                            for ii2, k2 in enumerate(ks2):
                                dmac(dstd2[:, :, k2, :],
                                     ts22[:, C * ii2:C * (ii2 + 1)].rearrange(
                                         "p (j r) -> p j r", r=128))
                        else:
                            dmac(dstd2[:, :, ks2[0], :],
                                 ts22[:, :width2].rearrange("p (m r) -> p m r", r=128))

                for si in range(2):
                    slab_load(si)
                emit_aw_dma(2)
                for si, (srcd, kind, dstd, ks, width) in enumerate(slabs):
                    stg = wstage[si % 3]
                    ts2 = tstg[si % 2]
                    flush_store()
                    quant_stage1(stg[:, :width], stg[:], width,
                                 qscp[si % 2], qrsp[si % 2])
                    emit_aw_act(2, aw_dpos[0])
                    if si + 2 < NS:
                        slab_load(si + 2)
                    emit_aw_dma(2)
                    quant_stage2(stg[:, :width], stg[:], width,
                                 si % 2, qscp[si % 2], qrsp[si % 2])
                    obt = qbR[si % 2]
                    for m in range(width // 128):
                        ps = psT[m % 2]
                        obv = obt[:, 128 * m:128 * (m + 1)].bitcast(BF16)
                        prog.op("pe", (lambda s, i_=obv, ps_=ps[:]:
                                       s.transpose(ps_, i_, ident[:])),
                                reads=[obt[:], ident[:]], writes=[ps[:]])
                        prog.op("act", (lambda s, o=ts2[:, 128 * m:128 * (m + 1)],
                                        ps_=ps[:]:
                                        s.activation(o, ps_, AF.Identity)),
                                reads=[ps[:]], writes=[ts2[:]])
                    pend_store.append((kind, dstd, ks, width, ts2))
                flush_store()
                while aw_apos[0] < len(aw_jobs):
                    emit_aw_dma(4)
                    emit_aw_act(4, aw_dpos[0])

                prog.barrier()

            x_bufs = [sbA(f"x_buf{i}", [128, 4, C], F32) for i in range(2)]
            h_toks = [sbA("h_tok0", [128, C], BF16)]
            h_cm = sbA("h_cm", [128, 8, 544], BF16)
            d_full = sbA("d_full", [128, 8, CHT], BF16)
            z_b = {w: sbA(f"z_{w}", [128, 8, CHT], BF16) for w in ("g", "r", "k", "v")}
            sg_full = sbA("sg_full", [128, 8, CHT], BF16)
            rw_full = sbA("rw_full", [128, 8, CHT], BF16)
            inner = sbA("inner", [128, 8, CHT], BF16)
            v_t = sbA("v_t", [128, CHT], F16)
            r_t = sbA("r_t", [128, CHT], BF16)
            kv_t = sbA("kv_t", [128, CHT], F32)
            wkv_t = sbA("wkv_t", [128, CHT], F32)
            rw2_t = [sbA(f"rw2_t{i}", [128, CHT], BF16) for i in range(2)]
            lnsum = sbA("lnsum", [128, 8])
            lnssq = sbA("lnssq", [128, 8])
            lnmean = sbA("lnmean", [128, 8])
            lnvar = sbA("lnvar", [128, 8])
            lnt1 = sbA("lnt1", [128, 8])
            lnrstd = sbA("lnrstd", [128, 8])
            lnnb = sbA("lnnb", [128, 8])
            gmean = sbA("gmean", [16, CHT])
            gvar = sbA("gvar", [16, CHT])
            gt1 = sbA("gt1", [16, CHT])
            grstd = sbA("grstd", [16, CHT])
            gmean_bf = gvar_holder = None

            psP = {w: pA.enter_context(nc.psum_tensor(f"ps_{w}", [128, CHT], F32))
                   for w in ("g", "r", "k", "v")}
            psSum = pA.enter_context(nc.psum_tensor("psSum", [16, CHT], F32))
            psSsq = pA.enter_context(nc.psum_tensor("psSsq", [16, CHT], F32))
            psMu = pA.enter_context(nc.psum_tensor("psMu", [128, CHT], F32))
            psRs = pA.enter_context(nc.psum_tensor("psRs", [128, CHT], F32))

            prog.op("act", lambda s: s.preload_act_table(AF.Sigmoid))


            def ln_stats_tile(xb, x_ap, col, junk_ap):
                prog.op("dve", (lambda s, o=lnsum[:, col:col + 1], x=x_ap:
                                s.reduce_sum(o, x, AX.X)),
                        reads=[xb[:]], writes=[lnsum[:]])
                prog.op("act", (lambda s, o=junk_ap[:, :512], x=x_ap[:, :512],
                                a=lnssq[:, col:col + 1]:
                                s.activation(o, x, AF.Square, accum_out=a)),
                        reads=[xb[:]], writes=[junk_ap, lnssq[:]])
                prog.op("act", (lambda s, o=junk_ap[:, 512:], x=x_ap[:, 512:],
                                a=lnsum[:, 4 + col:5 + col]:
                                s.activation(o, x, AF.Square, accum_out=a)),
                        reads=[xb[:]], writes=[lnsum[:]])
                prog.op("dve", (lambda s, o=lnssq[:, col:col + 1],
                                b=lnsum[:, 4 + col:5 + col]:
                                s.tensor_tensor(o, o, b, ALU.add)),
                        reads=[lnssq[:], lnsum[:]], writes=[lnssq[:]],
                        selfsync=True)

            def ln_finish(ncols):
                sl = slice(0, ncols)
                ms = lnmean[:, sl]; vs = lnvar[:, sl]
                t1 = lnt1[:, sl]
                ys = lnrstd[:, sl]; nb_ = lnnb[:, sl]
                prog.op("dve", (lambda s: s.tensor_scalar(ms, lnsum[:, sl], 1.0 / C, None, ALU.mult)),
                        reads=[lnsum[:]], writes=[lnmean[:]], selfsync=True)
                prog.op("dve", (lambda s: s.tensor_tensor(t1, ms, ms, ALU.mult)),
                        reads=[lnmean[:]], writes=[lnt1[:]], selfsync=True)
                prog.op("dve", (lambda s: s.scalar_tensor_tensor(vs, lnssq[:, sl], 1.0 / C, t1, ALU.mult, ALU.subtract)),
                        reads=[lnssq[:], lnt1[:]], writes=[lnvar[:]], selfsync=True)
                prog.op("dve", (lambda s: s.tensor_scalar(vs, vs, EPS, None, ALU.add)),
                        reads=[lnvar[:]], writes=[lnvar[:]], selfsync=True)
                quake_rsqrt("dve", vs, ys, t1, lnvar[:], lnrstd[:], lnt1[:])
                prog.op("dve", (lambda s: s.scalar_tensor_tensor(nb_, ms, -1.0, ys, ALU.mult, ALU.mult)),
                        reads=[lnmean[:], lnrstd[:]], writes=[lnnb[:]], selfsync=True)

            def ln_apply(xb, x_ap, out_ap, col):
                prog.op("act", (lambda s, o=out_ap, x=x_ap,
                                b=lnnb[:, col:col + 1], sc=lnrstd[:, col:col + 1]:
                                s.activation(o, x, AF.Identity, bias=b, scale=sc)),
                        reads=[xb[:], lnnb[:], lnrstd[:]], writes=[out_ap])

            dma(x_bufs[0][:, 0, :], X_t[:, NT - 1, :])
            ln_stats_tile(x_bufs[0], x_bufs[0][:, 0, :], 0, h_toks[0][:])
            ln_finish(1)
            ln_apply(x_bufs[0], x_bufs[0][:, 0, :], h_toks[0][:], 0)
            hlast = d_full[:, :, :128]
            dmat(hlast, h_toks[0][:])
            prog.op("dve", lambda s: s.tensor_copy(h_cm[:, :, 15:16], hlast[:, :, 127:128]),
                    reads=[d_full[:]], writes=[h_cm[:]])

            def p1_head(n):
                xb = x_bufs[n % 2]
                ht = h_toks[0]
                for a in range(4):
                    dma(xb[:, a, :], X_t[:, 4 * n + a, :])
                    ln_stats_tile(xb, xb[:, a, :], a, ht[:])
                ln_finish(4)
                if n > 0:
                    prog.op("dve", lambda s: s.tensor_copy(h_cm[:, :, 15:16], h_cm[:, :, 527:528]),
                            reads=[h_cm[:]], writes=[h_cm[:]])
                for a in range(4):
                    ln_apply(xb, xb[:, a, :], ht[:], a)
                    off = 16 + 128 * a
                    dmat(h_cm[:, :, off:off + 128], ht[:])

            def p1_mixd(n):
                for j in range(CT):
                    prog.op("dve", (lambda s, o=d_full[:, j, :], hj=h_cm[:, j, 16:528],
                                    hsj=h_cm[:, j, 15:527]:
                                    s.tensor_tensor(o, hj, hsj, ALU.subtract)),
                            reads=[h_cm[:]], writes=[d_full[:]], selfsync=True)

            def p1_mixz(n):
                for j in range(CT):
                    hs_j = h_cm[:, j, 15:527]
                    for w in ("g", "r", "k", "v"):
                        prog.op("act", (lambda s, zj=z_b[w][:, j, :], dj=d_full[:, j, :],
                                        mw=m_c[w][:, j:j + 1]:
                                        s.activation(zj, dj, AF.Identity, scale=mw)),
                                reads=[d_full[:], m_c[w][:]], writes=[z_b[w][:]], selfsync=True)
                        prog.op("pool", (lambda s, zj=z_b[w][:, j, :], hsj=hs_j:
                                         s.tensor_tensor(zj, zj, hsj, ALU.add)),
                                reads=[h_cm[:]], writes=[z_b[w][:]], selfsync=True)

            def p1_proj(n):
                pend = []
                for j in range(CT):
                    for w, wn in (("v", "wv"), ("k", "wk"), ("g", "wg"), ("r", "wr")):
                        for k in range(CT):
                            prog.op("pe", (lambda s, ps=psP[w][:], lt=wT_b[wn][:, k, 128 * j:128 * (j + 1)],
                                           rh=z_b[w][:, k, :], st=(k == 0), sp=(k == CT - 1):
                                           s.matmul(ps, lt, rh, start=st, stop=sp)),
                                    reads=[wT_b[wn][:], z_b[w][:]],
                                    writes=[psP[w][:]] if k in (0, CT - 1) else [])
                    prog.op("act", (lambda s, ps=psP["v"][:], b=b_c["v"][:, j:j + 1]:
                                    s.activation(v_t[:], ps, AF.Identity, bias=b)),
                            reads=[psP["v"][:], b_c["v"][:]], writes=[v_t[:]])
                    prog.op("dve", (lambda s, ps=psP["k"][:], b=b_c["k"][:, j:j + 1]:
                                    s.scalar_tensor_tensor(kv_t[:], ps, b, v_t[:], ALU.add, ALU.mult)),
                            reads=[psP["k"][:], b_c["k"][:], v_t[:]], writes=[kv_t[:]])
                    prog.op("act", (lambda s, o=sg_full[:, j, :], ps=psP["g"][:], b=b_c["g"][:, j:j + 1]:
                                    s.activation(o, ps, AF.Sigmoid, bias=b)),
                            reads=[psP["g"][:], b_c["g"][:]], writes=[sg_full[:]])
                    prog.op("act", (lambda s, ps=psP["r"][:], b=b_c["r"][:, j:j + 1]:
                                    s.activation(r_t[:], ps, AF.Identity, bias=b)),
                            reads=[psP["r"][:], b_c["r"][:]], writes=[r_t[:]])
                    prog.op("dve", (lambda s, d0=decay_c[:, j:j + 1].broadcast_to([128, CHT]),
                                    init=carry[:, j:j + 1]:
                                    s.tensor_tensor_scan(wkv_t[:], d0, kv_t[:], init, ALU.mult, ALU.add)),
                            reads=[decay_c[:], kv_t[:], carry[:]], writes=[wkv_t[:]])
                    prog.op("dve", (lambda s, cc=carry[:, j:j + 1]:
                                    s.tensor_copy(cc, wkv_t[:, CHT - 1:CHT])),
                            reads=[wkv_t[:]], writes=[carry[:]], selfsync=True)
                    prog.op("dve", (lambda s, u=u_c[:, j:j + 1]:
                                    s.scalar_tensor_tensor(wkv_t[:], kv_t[:], u, wkv_t[:], ALU.mult, ALU.add)),
                            reads=[kv_t[:], u_c[:], wkv_t[:]], writes=[wkv_t[:]])
                    prog.op("pool", (lambda s, o=rw_full[:, j, :]:
                                     s.tensor_tensor(o, r_t[:], wkv_t[:], ALU.mult)),
                            reads=[r_t[:], wkv_t[:]], writes=[rw_full[:]])
                    prog.op("pool", (lambda s, o=rw2_t[j % 2][:], i_=rw_full[:, j, :]:
                                     s.tensor_tensor(o, i_, i_, ALU.mult)),
                            reads=[rw_full[:]], writes=[rw2_t[j % 2][:]],
                            selfsync=True)
                    pend.append(j)
                    if len(pend) > 1:
                        emit_stat_mm(pend.pop(0))
                while pend:
                    emit_stat_mm(pend.pop(0))

            def emit_stat_mm(j):
                prog.op("pe", (lambda s, lt=gnsel[:, j, :], i_=rw_full[:, j, :],
                               st=(j == 0), sp=(j == CT - 1):
                               s.matmul(psSum[:], lt, i_, start=st, stop=sp)),
                        reads=[gnsel[:], rw_full[:]],
                        writes=[psSum[:]] if j in (0, CT - 1) else [])
                prog.op("pe", (lambda s, lt=gnsel[:, j, :], i_=rw2_t[j % 2][:],
                               st=(j == 0), sp=(j == CT - 1):
                               s.matmul(psSsq[:], lt, i_, start=st, stop=sp)),
                        reads=[gnsel[:], rw2_t[j % 2][:]],
                        writes=[psSsq[:]] if j in (0, CT - 1) else [])

            def p1_gn(n):
                prog.op("dve", lambda s: s.tensor_scalar(gmean[:], psSum[:], 1.0 / 64, None, ALU.mult),
                        reads=[psSum[:]], writes=[gmean[:]])
                prog.op("dve", lambda s: s.tensor_tensor(gt1[:], gmean[:], gmean[:], ALU.mult),
                        reads=[gmean[:]], writes=[gt1[:]])
                prog.op("dve", lambda s: s.scalar_tensor_tensor(gvar[:], psSsq[:], 1.0 / 64, gt1[:], ALU.mult, ALU.subtract),
                        reads=[psSsq[:], gt1[:]], writes=[gvar[:]])
                prog.op("dve", lambda s: s.tensor_scalar(gvar[:], gvar[:], EPS, None, ALU.add),
                        reads=[gvar[:]], writes=[gvar[:]])
                quake_rsqrt("dve", gvar[:], grstd[:], gt1[:],
                            gvar[:], grstd[:], gt1[:])
                gmean_bf = gvar[:].bitcast(BF16)[:, :CHT]
                grstd_bf = gt1[:].bitcast(BF16)[:, :CHT]
                prog.op("dve", lambda s, o=gmean_bf: s.tensor_copy(o, gmean[:]),
                        reads=[gmean[:]], writes=[gvar[:]], selfsync=True)
                prog.op("dve", lambda s, o=grstd_bf: s.tensor_copy(o, grstd[:]),
                        reads=[grstd[:]], writes=[gt1[:]], selfsync=True)
                for j in range(CT):
                    prog.op("pe", (lambda s, lt=gnbsel[:, j, :], rh=gmean_bf:
                                   s.matmul(psMu[:], lt, rh, start=True, stop=True)),
                            reads=[gnbsel[:], gvar[:]], writes=[psMu[:]])
                    prog.op("pe", (lambda s, lt=gnbsel[:, j, :], rh=grstd_bf:
                                   s.matmul(psRs[:], lt, rh, start=True, stop=True)),
                            reads=[gnbsel[:], gt1[:]], writes=[psRs[:]])
                    prog.op("dve", (lambda s, rw=rw_full[:, j, :]:
                                    s.tensor_tensor(kv_t[:], rw, psMu[:], ALU.subtract)),
                            reads=[rw_full[:], psMu[:]], writes=[kv_t[:]])
                    prog.op("dve", lambda s: s.tensor_tensor(wkv_t[:], kv_t[:], psRs[:], ALU.mult),
                            reads=[kv_t[:], psRs[:]], writes=[wkv_t[:]])
                    prog.op("pool", (lambda s, o=inner[:, j, :], sg=sg_full[:, j, :]:
                                     s.tensor_tensor(o, wkv_t[:], sg, ALU.mult)),
                            reads=[wkv_t[:], sg_full[:]], writes=[inner[:]])

            WO_ROT = ("g", "r", "k", "v")

            def p1_wo(n):
                xb = x_bufs[n % 2]

                def res1(i):
                    a, hh = divmod(i, 2)
                    ps = psP[WO_ROT[i % 4]]
                    prog.op("dve", (lambda s, ps_=ps[:], x=xb[:, a, 512 * hh:512 * (hh + 1)]:
                                    s.tensor_tensor(x, x, ps_, ALU.add)),
                            reads=[xb[:], ps[:]], writes=[xb[:]])
                    if hh == 1:
                        dma(OUT_t[:, 4 * n + a, :], xb[:, a, :])

                for i in range(8):
                    a, hh = divmod(i, 2)
                    ps = psP[WO_ROT[i % 4]]
                    for j in range(CT):
                        prog.op("pe", (lambda s, ps_=ps[:], lt=inner[:, j, 128 * a:128 * (a + 1)],
                                       rh=wT_b["wo"][:, j, 512 * hh:512 * (hh + 1)], st=(j == 0):
                                       s.matmul(ps_, lt, rh, start=st, stop=False)),
                                reads=[inner[:], wT_b["wo"][:]],
                                writes=[ps[:]] if j == 0 else [])
                    prog.op("pe", (lambda s, ps_=ps[:], rh=bo_bf[0:1, 512 * hh:512 * (hh + 1)]:
                                   s.matmul(ps_, ones1[:], rh, start=False, stop=True)),
                            reads=[ones1[:], bo_bf[:]], writes=[ps[:]])
                    if i >= 3:
                        res1(i - 3)
                for i in range(5, 8):
                    res1(i)

            p1_head(0)
            p1_mixd(0)
            p1_mixz(0)
            for n in range(NCH):
                p1_proj(n)
                if n + 1 < NCH:
                    p1_head(n + 1)
                    p1_mixd(n + 1)
                p1_gn(n)
                if n + 1 < NCH:
                    p1_mixz(n + 1)
                p1_wo(n)

            prog.barrier()

        if stop_after >= 2:
          with ExitStack() as pB:
            def sbB(name, shape, dtype=F32):
                return pB.enter_context(nc.sbuf_tensor(name, shape, dtype))
            wgT2 = [sbB(f"wgT2_{j}", [128, II], BF16) for j in range(CT)]
            wuT2 = [sbB(f"wuT2_{j}", [128, II], BF16) for j in range(CT)]
            x1_buf = sbB("x1_buf", [128, 4, C], F32)
            x1_buf2 = sbB("x1_buf2", [128, 4, C], F32)
            x2_t2 = [sbB("x2t2_0", [128, C], F32)]
            qb2p = {"x": tuple(sbB(f"qb2{nm}x", [128, C], F16) for nm in "AMH"),
                    "h": tuple(sbB(f"qb2{nm}h", [128, II], F16) for nm in "AMH")}
            qb2R = {"x": [sbB(f"qb2R{i}x", [128, C], F16) for i in range(2)],
                    "h": [sbB(f"qb2R{i}h", [128, II], F16) for i in range(2)]}
            qsc2p = {"x": sbB("qsc2_x", [128, C // 16]),
                     "h": sbB("qsc2_h", [128, II // 16])}
            qrs2p = {"x": sbB("qrs2_x", [128, C // 16]),
                     "h": sbB("qrs2_h", [128, II // 16])}
            xq_cm = [sbB(f"xq_cm{i}", [128, 8, 128], BF16) for i in range(2)]
            sil_t = [sbB(f"sil_t{i}", [128, 512], F16) for i in range(2)]
            up_t = [sbB(f"up_t{i}", [128, 512], F16) for i in range(2)]
            h16 = [sbB(f"h16_{i}", [128, II], F16) for i in range(3)]
            l2sum = sbB("l2sum", [128, 8])
            l2ssq = sbB("l2ssq", [128, 8])
            l2mean = sbB("l2mean", [128, 8])
            l2var = sbB("l2var", [128, 8])
            l2t1 = sbB("l2t1", [128, 8])
            l2rstd = sbB("l2rstd", [128, 8])
            l2nb = sbB("l2nb", [128, 8])
            l2ssqB = sbB("l2ssqB", [128, 8])
            l2nbs = sbB("l2nbs", [128, 8])
            l2rstds = sbB("l2rstds", [128, 8])
            psGt = [pB.enter_context(nc.psum_tensor(f"psGt{i}", [128, 512], F32))
                    for i in range(2)]
            psUp = [pB.enter_context(nc.psum_tensor(f"psUp{i}", [128, 512], F32))
                    for i in range(2)]

            prog.op("act", lambda s: s.preload_act_table(AF.Silu))


            def emit_quant2(src_ap, src_full, width, par, bi):
                bA, bM, bH = qb2p[par]
                bR = qb2R[par][bi]
                qsc2 = qsc2p[par]
                qrs2 = qrs2p[par]
                nb = width // 16
                sc = qsc2[:, :nb]; rs = qrs2[:, :nb]
                src3 = src_ap.rearrange("p (b k) -> p b k", k=16)
                prog.op("dve", (lambda s, o=sc, x=src3: s.tensor_reduce(
                    o, x, AX.X, ALU.max, apply_absolute_value=True)),
                    reads=[src_full], writes=[qsc2[:]])
                prog.op("dve", (lambda s, o=sc: s.tensor_scalar(
                    o, o, 1.0 / 6.0, 1e-12, ALU.mult, ALU.max)),
                    reads=[qsc2[:]], writes=[qsc2[:]], selfsync=True)
                prog.op("dve", (lambda s, o=rs, i_=sc: s.reciprocal(o, i_)),
                        reads=[qsc2[:]], writes=[qrs2[:]], selfsync=True)
                xn = bA[:, :width]
                rsb = rs.unsqueeze(2).broadcast_to([128, nb, 16])
                scb = sc.unsqueeze(2).broadcast_to([128, nb, 16])
                xn3 = xn.rearrange("p (b k) -> p b k", k=16)
                prog.op("dve", (lambda s, o=xn3, x=src3, r=rsb:
                                s.tensor_tensor(o, x, r, ALU.mult)),
                        reads=[src_full, qrs2[:]], writes=[bA[:]])
                t = xn.bitcast(U16)
                br = bR[:, :width].bitcast(U16)
                bm = bM[:, :width].bitcast(U16)
                bh = bH[:, :width].bitcast(U16)
                prog.op("dve", (lambda s: s.tensor_single_scalar(bm, t, 0x7FFF, ALU.bitwise_and)),
                        reads=[bA[:]], writes=[bM[:]])
                prog.op("dve", (lambda s: s.tensor_single_scalar(br, t, 0x0100, ALU.add)),
                        reads=[bA[:]], writes=[bR[:]])
                prog.op("dve", (lambda s: s.tensor_single_scalar(br, br, 0x7E00, ALU.bitwise_and)),
                        reads=[bR[:]], writes=[bR[:]], selfsync=True)
                prog.op("dve", (lambda s: s.tensor_scalar(br, br, 0x3C00, 0x3800, ALU.max, ALU.subtract)),
                        reads=[bR[:]], writes=[bR[:]], selfsync=True)
                prog.op("dve", (lambda s: s.tensor_scalar(bh, bm, 0x3400, 0x3800, ALU.is_gt, ALU.mult)),
                        reads=[bM[:]], writes=[bH[:]])
                prog.op("dve", (lambda s: s.scalar_tensor_tensor(br, bm, 0x3A00, br, ALU.is_gt, ALU.mult)),
                        reads=[bM[:], bR[:]], writes=[bR[:]])
                prog.op("dve", (lambda s: s.tensor_tensor(br, br, bh, ALU.add)),
                        reads=[bR[:], bH[:]], writes=[bR[:]], selfsync=True)
                prog.op("dve", (lambda s: s.tensor_single_scalar(bm, t, 0x8000, ALU.bitwise_and)),
                        reads=[bA[:]], writes=[bM[:]])
                prog.op("dve", (lambda s: s.tensor_tensor(br, bm, br, ALU.bitwise_or)),
                        reads=[bM[:], bR[:]], writes=[bR[:]])
                q3 = bR[:, :width].rearrange("p (b k) -> p b k", k=16)
                ob3 = bR[:, :width].bitcast(BF16).rearrange("p (b k) -> p b k", k=16)
                prog.op("pool", (lambda s, o=ob3, q=q3, ss=scb:
                                 s.tensor_tensor(o, q, ss, ALU.mult)),
                        reads=[bR[:], qsc2[:]], writes=[bR[:]])

            WG4 = WGall.rearrange("p j k r -> p j (k r)")
            WU4 = WUall.rearrange("p j k r -> p j (k r)")

            IC = [(0, 512), (512, 512), (1024, 512), (1536, 512), (2048, 512), (2560, 256)]

            def ln2_finish(ncols):
                sl = slice(0, ncols)
                ms = l2mean[:, sl]; vs = l2var[:, sl]
                t1 = l2t1[:, sl]
                ys = l2rstd[:, sl]; nb_ = l2nb[:, sl]
                prog.op("dve", (lambda s: s.tensor_scalar(ms, l2sum[:, sl], 1.0 / C, None, ALU.mult)),
                        reads=[l2sum[:]], writes=[l2mean[:]], selfsync=True)
                prog.op("dve", (lambda s: s.tensor_tensor(t1, ms, ms, ALU.mult)),
                        reads=[l2mean[:]], writes=[l2t1[:]], selfsync=True)
                prog.op("dve", (lambda s: s.scalar_tensor_tensor(vs, l2ssq[:, sl], 1.0 / C, t1, ALU.mult, ALU.subtract)),
                        reads=[l2ssq[:], l2t1[:]], writes=[l2var[:]], selfsync=True)
                prog.op("dve", (lambda s: s.tensor_scalar(vs, vs, EPS, None, ALU.add)),
                        reads=[l2var[:]], writes=[l2var[:]], selfsync=True)
                quake_rsqrt("dve", vs, ys, t1, l2var[:], l2rstd[:], l2t1[:])
                prog.op("dve", (lambda s: s.scalar_tensor_tensor(nb_, ms, -1.0, ys, ALU.mult, ALU.mult)),
                        reads=[l2mean[:], l2rstd[:]], writes=[l2nb[:]], selfsync=True)

            x1_bufs = [x1_buf, x1_buf2]

            def p2a_load(n):
                for a in range(4):
                    dma(x1_bufs[n % 2][:, a, :], OUT_t[:, 4 * n + a, :])

            def p2a_prep(n):
                xb = x1_bufs[n % 2]
                for aa in range(4):
                    prog.op("dve", (lambda s, o=l2sum[:, aa:aa + 1], x=xb[:, aa, :]:
                                    s.reduce_sum(o, x, AX.X)),
                            reads=[xb[:]], writes=[l2sum[:]])
                    jk = x2_t2[0]
                    prog.op("act", (lambda s, x=xb[:, aa, :512], acc=l2ssq[:, aa:aa + 1],
                                    o=jk[:, :512]:
                                    s.activation(o, x, AF.Square, accum_out=acc)),
                            reads=[xb[:]], writes=[jk[:], l2ssq[:]])
                    prog.op("act", (lambda s, x=xb[:, aa, 512:], acc=l2ssqB[:, aa:aa + 1],
                                    o=jk[:, 512:]:
                                    s.activation(o, x, AF.Square, accum_out=acc)),
                            reads=[xb[:]], writes=[jk[:], l2ssqB[:]])
                    prog.op("dve", (lambda s, o=l2ssq[:, aa:aa + 1], b=l2ssqB[:, aa:aa + 1]:
                                    s.tensor_tensor(o, o, b, ALU.add)),
                            reads=[l2ssq[:], l2ssqB[:]], writes=[l2ssq[:]],
                            selfsync=True)
                ln2_finish(4)
                prog.op("dve", (lambda s, o=l2nbs[:, 4 * (n % 2):4 * (n % 2) + 4]:
                                s.tensor_copy(o, l2nb[:, 0:4])),
                        reads=[l2nb[:]], writes=[l2nbs[:]], selfsync=True)
                prog.op("dve", (lambda s, o=l2rstds[:, 4 * (n % 2):4 * (n % 2) + 4]:
                                s.tensor_copy(o, l2rstd[:, 0:4])),
                        reads=[l2rstd[:]], writes=[l2rstds[:]], selfsync=True)

            def p2a_front(g):
                n, a = divmod(g, 4)
                xb = x1_bufs[n % 2]
                pa = g % 2
                x2a = x2_t2[0]
                col = 4 * (n % 2) + a
                prog.op("act", (lambda s, o=x2a[:], x=xb[:, a, :],
                                b=l2nbs[:, col:col + 1], sc=l2rstds[:, col:col + 1]:
                                s.activation(o, x, AF.Identity, bias=b, scale=sc)),
                        reads=[xb[:], l2nbs[:], l2rstds[:]], writes=[x2a[:]])
                emit_quant2(x2a[:], x2a[:], C, "x", pa)
                xqv = qb2R["x"][pa][:, :C].bitcast(BF16)
                dmat(xq_cm[pa][:], xqv)

            def p2a_mid(g):
                pa = g % 2
                xca = xq_cm[pa]
                h16a = h16[g % 3]
                for ici, (io, iw) in enumerate(IC):
                    pg = psGt[ici % 2]
                    pu = psUp[ici % 2]
                    sl = sil_t[ici % 2]
                    for j in range(CT):
                        prog.op("pe", (lambda s, lt=xca[:, j, :], rh=wgT2[j][:, io:io + iw],
                                       ps=pg[:, :iw], st=(j == 0), sp=(j == CT - 1):
                                       s.matmul(ps, lt, rh, start=st, stop=sp)),
                                reads=[xca[:], wgT2[j][:]],
                                writes=[pg[:]] if j in (0, CT - 1) else [])
                    for j in range(CT):
                        prog.op("pe", (lambda s, lt=xca[:, j, :], rh=wuT2[j][:, io:io + iw],
                                       ps=pu[:, :iw], st=(j == 0), sp=(j == CT - 1):
                                       s.matmul(ps, lt, rh, start=st, stop=sp)),
                                reads=[xca[:], wuT2[j][:]],
                                writes=[pu[:]] if j in (0, CT - 1) else [])
                    prog.op("act", (lambda s, o=sl[:, :iw], ps=pg[:, :iw]:
                                    s.activation(o, ps, AF.Silu)),
                            reads=[pg[:]], writes=[sl[:]])
                    up = up_t[ici % 2]
                    prog.op("act", (lambda s, o=up[:, :iw], ps=pu[:, :iw]:
                                    s.activation(o, ps, AF.Identity)),
                            reads=[pu[:]], writes=[up[:]])
                    prog.op("pool", (lambda s, o=h16a[:, io:io + iw], si=sl[:, :iw], u=up[:, :iw]:
                                     s.tensor_tensor(o, si, u, ALU.mult)),
                            reads=[sl[:], up[:]], writes=[h16a[:]])

            hq_pend = []

            def p2a_back(g):
                while hq_pend:
                    g2 = hq_pend.pop(0)
                    dmac(HQ_t[:, g2, :], qb2R["h"][g2 % 2][:, :II].bitcast(BF16))
                emit_quant2(h16[g % 3][:], h16[g % 3][:], II, "h", g % 2)
                hq_pend.append(g)

            p2a_load(0)
            p2a_prep(0)
            for j in range(CT):
                dmac(wgT2[j][:], WG4[:, j, :])
                dmac(wuT2[j][:], WU4[:, j, :])
            p2a_front(0)
            for g in range(NT):
                if g % 4 == 1 and g // 4 + 1 < NCH:
                    p2a_load(g // 4 + 1)
                if g % 4 == 3 and g // 4 + 1 < NCH:
                    p2a_prep(g // 4 + 1)
                if g + 1 < NT:
                    p2a_front(g + 1)
                p2a_mid(g)
                if g >= 2:
                    p2a_back(g - 2)
            p2a_back(NT - 2)
            p2a_back(NT - 1)
            while hq_pend:
                g2 = hq_pend.pop(0)
                dmac(HQ_t[:, g2, :], qb2R["h"][g2 % 2][:, :II].bitcast(BF16))

            prog.barrier()

        if stop_after >= 3:
          with ExitStack() as pC:
            def sbC(name, shape, dtype=F32):
                return pC.enter_context(nc.sbuf_tensor(name, shape, dtype))
            wdT = sbC("wdT", [128, IT, C], BF16)
            hqT = [sbC(f"hqT{i}", [128, IT, 4, 128], BF16) for i in range(2)]
            x1b = [sbC(f"x1b{i}", [128, 4, C], F32) for i in range(2)]
            psD = [pC.enter_context(nc.psum_tensor(f"psD{i}", [128, 512], F32))
                   for i in range(8)]

            hstg = [sbC(f"hstg{i}", [128, II], BF16) for i in range(2)]
            dmac(wdT[:], WDall.rearrange("p m k r -> p m (k r)"))

            def p2b_prefetch(n):
                for a in range(4):
                    stg = hstg[a % 2]
                    dmac(stg[:], HQd[512 * n + 128 * a:512 * n + 128 * (a + 1), :])
                    dmat(hqT[n % 2][:, :, a, :], stg[:])
                for a in range(4):
                    dma(x1b[n % 2][:, a, :], OUT_t[:, 4 * n + a, :])

            p2b_prefetch(0)
            for n in range(NCH):
                hq_n = hqT[n % 2]
                x1_n = x1b[n % 2]
                for m in range(IT):
                    for a in range(4):
                        for hh in range(2):
                            prog.op("pe", (lambda s, ps=psD[a * 2 + hh][:],
                                           lt=hq_n[:, m, a, :],
                                           rh=wdT[:, m, 512 * hh:512 * (hh + 1)],
                                           st=(m == 0), sp=(m == IT - 1):
                                           s.matmul(ps, lt, rh, start=st, stop=sp)),
                                    reads=[hq_n[:], wdT[:]],
                                    writes=[psD[a * 2 + hh][:]] if m in (0, IT - 1) else [])
                if n + 1 < NCH:
                    p2b_prefetch(n + 1)
                for a in range(4):
                    for hh in range(2):
                        prog.op("dve", (lambda s, x=x1_n[:, a, 512 * hh:512 * (hh + 1)],
                                        ps=psD[a * 2 + hh][:]:
                                        s.tensor_tensor(x, x, ps, ALU.add)),
                                reads=[x1_n[:], psD[a * 2 + hh][:]], writes=[x1_n[:]])
                    dma(OUT_t[:, 4 * n + a, :], x1_n[:, a, :])

            prog.barrier()

        with nc.Block("main") as blk:
            prog.emit_all(blk, sems)

    return nc


_NC_CACHE = None


def kernel(**inputs):
    global _NC_CACHE
    f32 = np.float32
    bf16 = ml_dtypes.bfloat16
    x = np.ascontiguousarray(inputs["L0_attn_x"], dtype=f32)

    def colmaj(v):
        return np.ascontiguousarray(np.asarray(v, f32).reshape(8, 128).T)

    base = {}
    for nm, wk in (("wg", "Wg"), ("wr", "Wr"), ("wk", "Wk"), ("wv", "Wv"), ("wo", "Wo")):
        base[f"{nm}T"] = np.ascontiguousarray(
            np.asarray(inputs[f"L0_attn_{wk}_w"], f32).T)
    for nm, wk in (("g", "Wg"), ("r", "Wr"), ("k", "Wk"), ("v", "Wv")):
        base[f"b{nm}"] = colmaj(inputs[f"L0_attn_{wk}_b"])
    base["bo_row"] = np.ascontiguousarray(
        np.asarray(inputs["L0_attn_Wo_b"], f32).reshape(1, C))
    for nm in ("g", "r", "k", "v"):
        base[f"m{nm}"] = colmaj(inputs[f"L0_attn_m{nm}"])
    base["w_vec"] = colmaj(np.asarray(inputs["L0_attn_w"], f32).reshape(-1))
    base["u_vec"] = colmaj(np.asarray(inputs["L0_attn_u"], f32).reshape(-1))
    base["zeros_c"] = np.zeros((128, 8), f32)
    base["ones1"] = np.ones((1, 128), bf16)
    base["ident"] = np.eye(128, dtype=bf16)

    gnsel = np.zeros((128, 8, 16), bf16)
    gnbsel = np.zeros((16, 8, 128), bf16)
    for j in range(8):
        for p in range(128):
            hh = 2 * j + p // 64
            gnsel[p, j, hh] = 1
            gnbsel[hh, j, p] = 1
    base["gnsel"] = np.ascontiguousarray(gnsel.reshape(128, 8 * 16))
    base["gnbsel"] = np.ascontiguousarray(gnbsel.reshape(16, 8 * 128))
    base["w_gate"] = np.ascontiguousarray(inputs["L0_ffn_w_gate"], dtype=f32)
    base["w_up"] = np.ascontiguousarray(inputs["L0_ffn_w_up"], dtype=f32)
    base["w_down"] = np.ascontiguousarray(inputs["L0_ffn_w_down"], dtype=f32)

    if _NC_CACHE is None:
        _NC_CACHE = build_nc()
    nc = _NC_CACHE

    in_maps = []
    for b in range(8):
        m = dict(base)
        m["x"] = np.ascontiguousarray(x[b])
        in_maps.append(m)

    res = run_bass_kernel_spmd(nc, in_maps, core_ids=list(range(8)))
    out = np.stack([np.asarray(res.results[b]["out"]) for b in range(8)], axis=0)
    return out.astype(np.float32)


# version 46
# speedup vs baseline: 1.3329x; 1.0367x over previous
import numpy as np
import ml_dtypes
from contextlib import ExitStack

import concourse.bass as bass
import concourse.mybir as mybir
from concourse.bass_utils import run_bass_kernel_spmd

F32 = mybir.dt.float32
F16 = mybir.dt.float16
BF16 = mybir.dt.bfloat16
U16 = mybir.dt.uint16
I32 = mybir.dt.int32
ALU = mybir.AluOpType
AF = mybir.ActivationFunctionType
AX = mybir.AxisListType

T = 2048
C = 1024
II = 2816
NT = 16
NCH = 4
CT = 8
IT = 22
CHT = 512
EPS = 1e-5
MAGIC = 1597463007.0

ENGINES = ("sp", "pe", "act", "dve", "pool")


class Prog:

    def __init__(self):
        self.ops = []
        self.eng_ops = {e: [] for e in ENGINES}
        self.writer = {}
        self.readers = {}
        self.last_dmat = None
        self.last_real = {e: None for e in ENGINES}
        self.tot = {"ld": 0, "st": 0}
        self.cur_ep = {"ld": 0, "st": 0}
        self.unsealed = {"ld": [], "st": []}
        self.ep_of = {}
        self.n_coll = 0

    @staticmethod
    def _key(ap):
        return ap.tensor.name

    def _seal(self, kl):
        if not self.unsealed[kl]:
            return
        self.cur_ep[kl] += 1
        for o in self.unsealed[kl]:
            self.ep_of[o] = (kl, self.cur_ep[kl])
        self.unsealed[kl] = []
        opid = len(self.ops)
        self.ops.append(dict(eng="sp", fn="EPOCH", kl=kl,
                             waits={("raw", kl, self.tot[kl])}, isdma=False))
        self.eng_ops["sp"].append(opid)

    def seal_all(self):
        self._seal("ld")
        self._seal("st")

    def op(self, eng, fn, reads=(), writes=(), dmat=False, after=(),
           selfsync=False, coll=False):
        waits = set()

        def dep(d):
            if d is None:
                return
            drec = self.ops[d]
            de = drec["eng"]
            if drec.get("coll"):
                waits.add(("cc", drec["coll_tot"]))
                return
            if drec["isdma"]:
                if eng == "sp":
                    waits.add(("raw", drec["kl"], drec["tot"]))
                else:
                    if d not in self.ep_of:
                        self._seal(drec["kl"])
                    waits.add(("ep",) + self.ep_of[d])
                return
            if de == eng and not selfsync:
                return
            waits.add(d)

        for a in reads:
            dep(self.writer.get(self._key(a)))
        for a in writes:
            dep(self.writer.get(self._key(a)))
            for r in self.readers.get(self._key(a), ()):
                dep(r)
        for d in after:
            dep(d)
        isdma = eng == "sp" and fn is not None
        kl = None
        if isdma:
            import concourse.bass as _bass
            spc = str(writes[0].space) if writes else "?"
            kl = "ld" if "SBUF" in spc or "PSUM" in spc else "st"
        if dmat:
            waits.add(("raw", "ld", self.tot["ld"]))
            waits.add(("raw", "st", self.tot["st"]))
        elif isdma and self.last_dmat is not None:
            waits.add(("raw", "ld", self.ops[self.last_dmat]["tot"]))
        opid = len(self.ops)
        rec = dict(eng=eng, fn=fn, waits=waits, isdma=isdma, kl=kl)
        if coll:
            self.n_coll += 1
            rec["coll"] = True
            rec["coll_tot"] = 16 * self.n_coll
        if isdma:
            self.tot[kl] += 16
            rec["tot"] = self.tot[kl]
            self.unsealed[kl].append(opid)
        if dmat:
            self.last_dmat = opid
        self.ops.append(rec)
        self.eng_ops[eng].append(opid)
        if fn is not None:
            self.last_real[eng] = opid
        for a in reads:
            self.readers.setdefault(self._key(a), []).append(opid)
        for a in writes:
            self.writer[self._key(a)] = opid
            self.readers[self._key(a)] = []
        return opid

    def barrier(self):
        self.seal_all()
        lasts = [self.last_real[e] for e in ENGINES if self.last_real[e] is not None]
        eps = dict(self.cur_ep)
        for e in ENGINES:
            oid = self.op(e, None, after=lasts)
            for kl in ("ld", "st"):
                if eps[kl]:
                    self.ops[oid]["waits"].add(("ep", kl, eps[kl]))

    def emit_all(self, block, sems):
        self.seal_all()
        marked = set()
        for rec in self.ops:
            for d in rec["waits"]:
                if isinstance(d, int):
                    marked.add(d)
        for oid in self.eng_ops["sp"]:
            if self.ops[oid]["isdma"]:
                marked.add(oid)
        val = {}
        for e in ENGINES:
            c = 0
            for oid in self.eng_ops[e]:
                if self.ops[oid].get("coll"):
                    continue
                if oid in marked:
                    assert self.ops[oid]["fn"] is not None, "wait on a no-op"
                    c += 16 if self.ops[oid]["isdma"] else 1
                    val[oid] = c
        streams = {
            "sp": block.sync, "pe": block.tensor, "act": block.scalar,
            "dve": block.vector, "pool": block.gpsimd,
        }
        semnames = list(ENGINES) + ["epl", "eps", "cc", "ld", "st"]
        epname = {"ld": "epl", "st": "eps"}

        def make(eng_name):
            oplist = self.eng_ops[eng_name]

            def emitter(eng_obj):
                waited = {s2: 0 for s2 in semnames}
                for oid in oplist:
                    rec = self.ops[oid]
                    need = {}
                    for d in rec["waits"]:
                        if isinstance(d, int):
                            de = self.ops[d]["eng"]
                            need[de] = max(need.get(de, 0), val[d])
                        elif d[0] == "ep":
                            nm = epname[d[1]]
                            need[nm] = max(need.get(nm, 0), d[2])
                        elif d[0] == "cc":
                            need["cc"] = max(need.get("cc", 0), d[1])
                        else:
                            assert eng_name == "sp"
                            nm = d[1]
                            need[nm] = max(need.get(nm, 0), d[2])
                    for se in semnames:
                        v = need.get(se, 0)
                        if v > waited[se]:
                            eng_obj.wait_ge(sems[se], v)
                            waited[se] = v
                    if rec["fn"] is None:
                        continue
                    if rec["fn"] == "EPOCH":
                        eng_obj.sem_inc(sems[epname[rec["kl"]]], 1)
                        continue
                    ins = rec["fn"](eng_obj)
                    if rec.get("coll"):
                        ins.then_inc(sems["cc"], 16)
                    elif oid in marked:
                        if rec["isdma"]:
                            ins.then_inc(sems[rec["kl"]], 16)
                        else:
                            ins.then_inc(sems[eng_name], 1)
            return emitter

        for e in ENGINES:
            if self.eng_ops[e]:
                streams[e](make(e))


def build_nc(stop_after=3, debug=False):
    nc = bass.Bass("TRN2", target_bir_lowering=False)

    Xd = nc.dram_tensor("x", [T, C], F32, kind="ExternalInput").ap()
    WT = {w: nc.dram_tensor(f"{w}T", [C, C], F32, kind="ExternalInput").ap()
          for w in ("wg", "wr", "wk", "wv", "wo")}
    Bv = {w: nc.dram_tensor(f"b{w}", [128, 8], F32, kind="ExternalInput").ap()
          for w in ("g", "r", "k", "v")}
    BOr = nc.dram_tensor("bo_row", [1, C], F32, kind="ExternalInput").ap()
    Mv = {w: nc.dram_tensor(f"m{w}", [128, 8], F32, kind="ExternalInput").ap()
          for w in ("g", "r", "k", "v")}
    Wvec = nc.dram_tensor("w_vec", [128, 8], F32, kind="ExternalInput").ap()
    Uvec = nc.dram_tensor("u_vec", [128, 8], F32, kind="ExternalInput").ap()
    Zc = nc.dram_tensor("zeros_c", [128, 8], F32, kind="ExternalInput").ap()
    Ones1 = nc.dram_tensor("ones1", [1, 128], BF16, kind="ExternalInput").ap()
    Ident = nc.dram_tensor("ident", [128, 128], BF16, kind="ExternalInput").ap()
    GSel = nc.dram_tensor("gnsel", [128, 8 * 16], BF16, kind="ExternalInput").ap()
    GBSel = nc.dram_tensor("gnbsel", [16, 8 * 128], BF16, kind="ExternalInput").ap()
    WGd = nc.dram_tensor("w_gate", [II, C], F32, kind="ExternalInput").ap()
    WUd = nc.dram_tensor("w_up", [II, C], F32, kind="ExternalInput").ap()
    WDd = nc.dram_tensor("w_down", [C, II], F32, kind="ExternalInput").ap()

    OUT = nc.dram_tensor("out", [T, C], F32, kind="ExternalOutput").ap()
    if debug:
        DBG = {
            "t_hcm": nc.dram_tensor("t_hcm", [128, 8 * 544], BF16, kind="ExternalOutput").ap(),
            "t_zg": nc.dram_tensor("t_zg", [128, 8 * 512], BF16, kind="ExternalOutput").ap(),
            "t_sg": nc.dram_tensor("t_sg", [128, 8 * 512], BF16, kind="ExternalOutput").ap(),
            "t_rw": nc.dram_tensor("t_rw", [128, 8 * 512], F32, kind="ExternalOutput").ap(),
            "t_inner": nc.dram_tensor("t_inner", [128, 8 * 512], BF16, kind="ExternalOutput").ap(),
            "t_gmean": nc.dram_tensor("t_gmean", [16, 512], F32, kind="ExternalOutput").ap(),
            "t_grstd": nc.dram_tensor("t_grstd", [16, 512], F32, kind="ExternalOutput").ap(),
        }
    WGall = nc.dram_tensor("wg_all", [128, 8, IT, 128], BF16, kind="Internal").ap()
    WUall = nc.dram_tensor("wu_all", [128, 8, IT, 128], BF16, kind="Internal").ap()
    WDall = nc.dram_tensor("wd_all", [128, IT, 8, 128], BF16, kind="Internal").ap()
    HQd = nc.dram_tensor("hq_d", [T, II], BF16, kind="Internal").ap()

    X_t = Xd.rearrange("(g p) c -> p g c", p=128)
    OUT_t = OUT.rearrange("(g p) c -> p g c", p=128)
    HQ_t = HQd.rearrange("(g p) i -> p g i", p=128)

    prog = Prog()

    with ExitStack() as outer:
        def osb(name, shape, dtype=F32):
            return outer.enter_context(nc.sbuf_tensor(name, shape, dtype))
        sems = {e: outer.enter_context(nc.semaphore(f"s_{e}")) for e in ENGINES}
        for nm in ("epl", "eps", "cc", "ld", "st"):
            sems[nm] = outer.enter_context(nc.semaphore(f"s_{nm}"))

        m_c = {w: osb(f"m_{w}", [128, 8]) for w in ("g", "r", "k", "v")}
        b_c = {w: osb(f"b_{w}", [128, 8]) for w in ("g", "r", "k", "v")}
        u_c = osb("u_c", [128, 8])
        w_c = osb("w_c", [128, 8])
        decay_c = osb("decay_c", [128, 8])
        carry = osb("carry", [128, 8])
        bo_bf = osb("bo_bf", [1, C], BF16)
        ones1 = osb("ones1_s", [1, 128], BF16)
        ident = osb("ident_s", [128, 128], BF16)
        gnsel = osb("gnsel_s", [128, 8, 16], BF16)
        gnbsel = osb("gnbsel_s", [16, 8, 128], BF16)

        def dma(dst, src, **kw):
            return prog.op("sp", lambda s: s.dma_start(out=dst, in_=src, **kw),
                           reads=[src], writes=[dst])

        def dmat(dst, src):
            return prog.op("sp", lambda s: s.dma_start_transpose(dst, src),
                           reads=[src], writes=[dst], dmat=True)

        def dmac(dst, src, **kw):
            return prog.op("pool", lambda s: s.dma_start(out=dst, in_=src, **kw),
                           reads=[src], writes=[dst], coll=True)

        def quake_rsqrt(eng, v, y, t1, vfull, yfull, t1full):
            prog.op(eng, lambda s: s.tensor_copy(t1, v.bitcast(I32)),
                    reads=[vfull], writes=[t1full], selfsync=True)
            prog.op(eng, lambda s: s.tensor_scalar(t1, t1, -0.5, MAGIC, ALU.mult, ALU.add),
                    reads=[t1full], writes=[t1full], selfsync=True)
            prog.op(eng, lambda s: s.tensor_copy(y.bitcast(I32), t1),
                    reads=[t1full], writes=[yfull], selfsync=True)
            for _ in range(2):
                prog.op(eng, lambda s: s.tensor_tensor(t1, y, y, ALU.mult),
                        reads=[yfull], writes=[t1full], selfsync=True)
                prog.op(eng, lambda s: s.tensor_tensor(t1, t1, v, ALU.mult),
                        reads=[t1full, vfull], writes=[t1full], selfsync=True)
                prog.op(eng, lambda s: s.tensor_scalar(t1, t1, -0.5, 1.5, ALU.mult, ALU.add),
                        reads=[t1full], writes=[t1full], selfsync=True)
                prog.op(eng, lambda s: s.tensor_tensor(y, y, t1, ALU.mult),
                        reads=[yfull, t1full], writes=[yfull], selfsync=True)

        with ExitStack() as pA:
            def sbA(name, shape, dtype=F32):
                return pA.enter_context(nc.sbuf_tensor(name, shape, dtype))
            wT_b = {w: sbA(f"{w}T_b", [128, 8, C], BF16)
                    for w in ("wg", "wr", "wk", "wv", "wo")}

            with ExitStack() as p0:
                def sb0(name, shape, dtype=F32):
                    return p0.enter_context(nc.sbuf_tensor(name, shape, dtype))

                bo_f = sb0("bo_f", [1, C])
                for w in ("g", "r", "k", "v"):
                    dma(m_c[w][:], Mv[w])
                    dma(b_c[w][:], Bv[w])
                dma(u_c[:], Uvec)
                dma(w_c[:], Wvec)
                dma(carry[:], Zc)
                dma(bo_f[:], BOr)
                dma(ones1[:], Ones1)
                dma(ident[:], Ident)
                dma(gnsel[:], GSel.rearrange("p (a b) -> p a b", b=16))
                dma(gnbsel[:], GBSel.rearrange("p (a b) -> p a b", b=128))

                prog.op("act", lambda s: s.preload_act_table(AF.Exp))
                prog.op("act", lambda s: s.activation(decay_c[:], w_c[:], AF.Exp),
                        reads=[w_c[:]], writes=[decay_c[:]])
                prog.op("act", lambda s: s.activation(decay_c[:], decay_c[:], AF.Exp, scale=-1.0),
                        reads=[decay_c[:]], writes=[decay_c[:]])
                prog.op("dve", lambda s: s.tensor_copy(bo_bf[:], bo_f[:]),
                        reads=[bo_f[:]], writes=[bo_bf[:]])

                wstage = [sb0(f"wstg{i}", [128, II], F32) for i in range(3)]
                awstg = [sb0(f"awstg{i}", [128, C], F32) for i in range(2)]
                aw_jobs = [(w, ct) for w in ("wg", "wr", "wk", "wv", "wo")
                           for ct in range(CT)]
                aw_dpos = [0]
                aw_apos = [0]

                def emit_aw_dma(nj):
                    for _ in range(nj):
                        k = aw_dpos[0]
                        if k >= len(aw_jobs):
                            return
                        w, ct = aw_jobs[k]
                        dma(awstg[k % 2][:], WT[w][128 * ct:128 * (ct + 1), :])
                        aw_dpos[0] += 1

                def emit_aw_act(nj, limit):
                    for _ in range(nj):
                        k = aw_apos[0]
                        if k >= limit:
                            return
                        w, ct = aw_jobs[k]
                        stg = awstg[k % 2]
                        prog.op("act", (lambda s, d=wT_b[w][:, ct, :], x=stg[:]:
                                        s.activation(d, x, AF.Identity)),
                                reads=[stg[:]], writes=[wT_b[w][:]])
                        aw_apos[0] += 1

                qbA = [sb0(f"qbA{i}", [128, II], F16) for i in range(2)]
                qbR = [sb0(f"qbR{i}", [128, II], F16) for i in range(2)]
                qbM = sb0("qbM", [128, II], F16)
                qbH = sb0("qbH", [128, II], F16)
                qscp = [sb0(f"qsc{i}", [128, II // 16], F32) for i in range(3)]
                qrsp = [sb0(f"qrs{i}", [128, II // 16], F32) for i in range(3)]
                psT = [p0.enter_context(nc.psum_tensor(f"psT{i}", [128, 128], BF16))
                       for i in range(2)]

                def quant_stage1(src_ap, src_full, width, scbuf, rsbuf):
                    nb = width // 16
                    sc = scbuf[:, :nb]
                    rs = rsbuf[:, :nb]
                    src3 = src_ap.rearrange("p (b k) -> p b k", k=16)
                    prog.op("dve", (lambda s, o=sc, x=src3: s.tensor_reduce(
                        o, x, AX.X, ALU.max, apply_absolute_value=True)),
                        reads=[src_full], writes=[scbuf[:]])
                    prog.op("dve", (lambda s, o=sc: s.tensor_scalar(
                        o, o, 1.0 / 6.0, 1e-12, ALU.mult, ALU.max)),
                        reads=[scbuf[:]], writes=[scbuf[:]], selfsync=True)
                    prog.op("dve", (lambda s, o=rs, i_=sc: s.reciprocal(o, i_)),
                            reads=[scbuf[:]], writes=[rsbuf[:]], selfsync=True)

                def pool_xn(si2):
                    _, _, _, _, width = slabs[si2]
                    stg = wstage[si2 % 3]
                    nb = width // 16
                    rs = qrsp[si2 % 3][:, :nb]
                    rsb = rs.unsqueeze(2).broadcast_to([128, nb, 16])
                    src3 = stg[:, :width].rearrange("p (b k) -> p b k", k=16)
                    xn3 = qbA[si2 % 2][:, :width].rearrange("p (b k) -> p b k", k=16)
                    prog.op("dve", (lambda s, o=xn3, x=src3, r=rsb:
                                    s.tensor_tensor(o, x, r, ALU.mult)),
                            reads=[stg[:], qrsp[si2 % 3][:]], writes=[qbA[si2 % 2][:]])

                def quant_stage2(width, bi, scbuf, rsbuf):
                    bAt = qbA[bi]
                    bRt = qbR[bi]
                    nb = width // 16
                    sc = scbuf[:, :nb]
                    scb = sc.unsqueeze(2).broadcast_to([128, nb, 16])
                    xn = bAt[:, :width]
                    t = xn.bitcast(U16)
                    br = bRt[:, :width].bitcast(U16)
                    bm = qbM[:, :width].bitcast(U16)
                    bh = qbH[:, :width].bitcast(U16)
                    prog.op("dve", (lambda s: s.tensor_single_scalar(bm, t, 0x7FFF, ALU.bitwise_and)),
                            reads=[bAt[:]], writes=[qbM[:]])
                    prog.op("dve", (lambda s: s.tensor_single_scalar(br, t, 0x0100, ALU.add)),
                            reads=[bAt[:]], writes=[bRt[:]])
                    prog.op("dve", (lambda s: s.tensor_single_scalar(br, br, 0x7E00, ALU.bitwise_and)),
                            reads=[bRt[:]], writes=[bRt[:]], selfsync=True)
                    prog.op("dve", (lambda s: s.tensor_scalar(br, br, 0x3C00, 0x3800, ALU.max, ALU.subtract)),
                            reads=[bRt[:]], writes=[bRt[:]], selfsync=True)
                    prog.op("dve", (lambda s: s.tensor_scalar(bh, bm, 0x3400, 0x3800, ALU.is_gt, ALU.mult)),
                            reads=[qbM[:]], writes=[qbH[:]])
                    prog.op("dve", (lambda s: s.scalar_tensor_tensor(br, bm, 0x3A00, br, ALU.is_gt, ALU.mult)),
                            reads=[qbM[:], bRt[:]], writes=[bRt[:]])
                    prog.op("dve", (lambda s: s.tensor_tensor(br, br, bh, ALU.add)),
                            reads=[bRt[:], qbH[:]], writes=[bRt[:]], selfsync=True)
                    prog.op("dve", (lambda s: s.tensor_single_scalar(bm, t, 0x8000, ALU.bitwise_and)),
                            reads=[bAt[:]], writes=[qbM[:]])
                    prog.op("dve", (lambda s: s.tensor_tensor(br, bm, br, ALU.bitwise_or)),
                            reads=[qbM[:], bRt[:]], writes=[bRt[:]])
                    q3 = bRt[:, :width].rearrange("p (b k) -> p b k", k=16)
                    ob3 = bRt[:, :width].bitcast(BF16).rearrange("p (b k) -> p b k", k=16)
                    prog.op("pool", (lambda s, o=ob3, q=q3, ss=scb:
                                     s.tensor_tensor(o, q, ss, ALU.mult)),
                            reads=[bRt[:], scbuf[:]], writes=[bRt[:]])

                slabs = ([(WGd, "gu", WGall, [2 * i, 2 * i + 1], 2048) for i in range(11)]
                         + [(WUd, "gu", WUall, [2 * i, 2 * i + 1], 2048) for i in range(11)]
                         + [(WDd, "d", WDall, [k], II) for k in range(CT)])
                tstg = [sb0(f"tstg{i}", [128, II], BF16) for i in range(2)]
                NS = len(slabs)

                def slab_load(si):
                    srcd, kind, dstd, ks, width = slabs[si]
                    stg = wstage[si % 3]
                    if kind == "gu":
                        for ii, k in enumerate(ks):
                            dma(stg[:, C * ii:C * (ii + 1)],
                                srcd[128 * k:128 * (k + 1), :])
                    else:
                        dma(stg[:, :width], srcd[128 * ks[0]:128 * (ks[0] + 1), :])

                pend_store = []

                def flush_store():
                    while pend_store:
                        kind2, dstd2, ks2, width2, ts22 = pend_store.pop(0)
                        if kind2 == "gu":
                            for ii2, k2 in enumerate(ks2):
                                dmac(dstd2[:, :, k2, :],
                                     ts22[:, C * ii2:C * (ii2 + 1)].rearrange(
                                         "p (j r) -> p j r", r=128))
                        else:
                            dmac(dstd2[:, :, ks2[0], :],
                                 ts22[:, :width2].rearrange("p (m r) -> p m r", r=128))

                for si in range(2):
                    slab_load(si)
                emit_aw_dma(2)
                w0 = slabs[0][4]
                quant_stage1(wstage[0][:, :w0], wstage[0][:], w0,
                             qscp[0], qrsp[0])
                pool_xn(0)
                for si, (srcd, kind, dstd, ks, width) in enumerate(slabs):
                    ts2 = tstg[si % 2]
                    flush_store()
                    emit_aw_act(2, aw_dpos[0])
                    if si + 1 < NS:
                        w1 = slabs[si + 1][4]
                        s1 = wstage[(si + 1) % 3]
                        quant_stage1(s1[:, :w1], s1[:], w1,
                                     qscp[(si + 1) % 3], qrsp[(si + 1) % 3])
                        pool_xn(si + 1)
                    if si + 2 < NS:
                        slab_load(si + 2)
                    emit_aw_dma(2)
                    quant_stage2(width, si % 2, qscp[si % 3], qrsp[si % 3])
                    obt = qbR[si % 2]
                    for m in range(width // 128):
                        ps = psT[m % 2]
                        obv = obt[:, 128 * m:128 * (m + 1)].bitcast(BF16)
                        prog.op("pe", (lambda s, i_=obv, ps_=ps[:]:
                                       s.transpose(ps_, i_, ident[:])),
                                reads=[obt[:], ident[:]], writes=[ps[:]])
                        prog.op("act", (lambda s, o=ts2[:, 128 * m:128 * (m + 1)],
                                        ps_=ps[:]:
                                        s.activation(o, ps_, AF.Identity)),
                                reads=[ps[:]], writes=[ts2[:]])
                    pend_store.append((kind, dstd, ks, width, ts2))
                flush_store()
                while aw_apos[0] < len(aw_jobs):
                    emit_aw_dma(4)
                    emit_aw_act(4, aw_dpos[0])

                prog.barrier()

            x_bufs = [sbA(f"x_buf{i}", [128, 4, C], F32) for i in range(2)]
            h_toks = [sbA("h_tok0", [128, C], BF16)]
            h_cm = sbA("h_cm", [128, 8, 544], BF16)
            d_full = sbA("d_full", [128, 8, CHT], BF16)
            z_b = {w: sbA(f"z_{w}", [128, 8, CHT], BF16) for w in ("g", "r", "k", "v")}
            sg_full = sbA("sg_full", [128, 8, CHT], BF16)
            rw_full = sbA("rw_full", [128, 8, CHT], BF16)
            inner = sbA("inner", [128, 8, CHT], BF16)
            v_t = sbA("v_t", [128, CHT], F16)
            r_t = sbA("r_t", [128, CHT], BF16)
            kv_t = sbA("kv_t", [128, CHT], F32)
            wkv_t = sbA("wkv_t", [128, CHT], F32)
            rw2_t = [sbA(f"rw2_t{i}", [128, CHT], BF16) for i in range(2)]
            lnsum = sbA("lnsum", [128, 8])
            lnssq = sbA("lnssq", [128, 8])
            lnmean = sbA("lnmean", [128, 8])
            lnvar = sbA("lnvar", [128, 8])
            lnt1 = sbA("lnt1", [128, 8])
            lnrstd = sbA("lnrstd", [128, 8])
            lnnb = sbA("lnnb", [128, 8])
            gmean = sbA("gmean", [16, CHT])
            gvar = sbA("gvar", [16, CHT])
            gt1 = sbA("gt1", [16, CHT])
            grstd = sbA("grstd", [16, CHT])
            gmean_bf = gvar_holder = None

            psP = {w: pA.enter_context(nc.psum_tensor(f"ps_{w}", [128, CHT], F32))
                   for w in ("g", "r", "k", "v")}
            psSum = pA.enter_context(nc.psum_tensor("psSum", [16, CHT], F32))
            psSsq = pA.enter_context(nc.psum_tensor("psSsq", [16, CHT], F32))
            psMu = pA.enter_context(nc.psum_tensor("psMu", [128, CHT], F32))
            psRs = pA.enter_context(nc.psum_tensor("psRs", [128, CHT], F32))

            prog.op("act", lambda s: s.preload_act_table(AF.Sigmoid))


            def ln_stats_tile(xb, x_ap, col, junk_ap):
                prog.op("dve", (lambda s, o=lnsum[:, col:col + 1], x=x_ap:
                                s.reduce_sum(o, x, AX.X)),
                        reads=[xb[:]], writes=[lnsum[:]])
                prog.op("act", (lambda s, o=junk_ap[:, :512], x=x_ap[:, :512],
                                a=lnssq[:, col:col + 1]:
                                s.activation(o, x, AF.Square, accum_out=a)),
                        reads=[xb[:]], writes=[junk_ap, lnssq[:]])
                prog.op("act", (lambda s, o=junk_ap[:, 512:], x=x_ap[:, 512:],
                                a=lnsum[:, 4 + col:5 + col]:
                                s.activation(o, x, AF.Square, accum_out=a)),
                        reads=[xb[:]], writes=[lnsum[:]])
                prog.op("dve", (lambda s, o=lnssq[:, col:col + 1],
                                b=lnsum[:, 4 + col:5 + col]:
                                s.tensor_tensor(o, o, b, ALU.add)),
                        reads=[lnssq[:], lnsum[:]], writes=[lnssq[:]],
                        selfsync=True)

            def ln_finish(ncols):
                sl = slice(0, ncols)
                ms = lnmean[:, sl]; vs = lnvar[:, sl]
                t1 = lnt1[:, sl]
                ys = lnrstd[:, sl]; nb_ = lnnb[:, sl]
                prog.op("dve", (lambda s: s.tensor_scalar(ms, lnsum[:, sl], 1.0 / C, None, ALU.mult)),
                        reads=[lnsum[:]], writes=[lnmean[:]], selfsync=True)
                prog.op("dve", (lambda s: s.tensor_tensor(t1, ms, ms, ALU.mult)),
                        reads=[lnmean[:]], writes=[lnt1[:]], selfsync=True)
                prog.op("dve", (lambda s: s.scalar_tensor_tensor(vs, lnssq[:, sl], 1.0 / C, t1, ALU.mult, ALU.subtract)),
                        reads=[lnssq[:], lnt1[:]], writes=[lnvar[:]], selfsync=True)
                prog.op("dve", (lambda s: s.tensor_scalar(vs, vs, EPS, None, ALU.add)),
                        reads=[lnvar[:]], writes=[lnvar[:]], selfsync=True)
                quake_rsqrt("dve", vs, ys, t1, lnvar[:], lnrstd[:], lnt1[:])
                prog.op("dve", (lambda s: s.scalar_tensor_tensor(nb_, ms, -1.0, ys, ALU.mult, ALU.mult)),
                        reads=[lnmean[:], lnrstd[:]], writes=[lnnb[:]], selfsync=True)

            def ln_apply(xb, x_ap, out_ap, col):
                prog.op("act", (lambda s, o=out_ap, x=x_ap,
                                b=lnnb[:, col:col + 1], sc=lnrstd[:, col:col + 1]:
                                s.activation(o, x, AF.Identity, bias=b, scale=sc)),
                        reads=[xb[:], lnnb[:], lnrstd[:]], writes=[out_ap])

            dma(x_bufs[0][:, 0, :], X_t[:, NT - 1, :])
            ln_stats_tile(x_bufs[0], x_bufs[0][:, 0, :], 0, h_toks[0][:])
            ln_finish(1)
            ln_apply(x_bufs[0], x_bufs[0][:, 0, :], h_toks[0][:], 0)
            hlast = d_full[:, :, :128]
            dmat(hlast, h_toks[0][:])
            prog.op("dve", lambda s: s.tensor_copy(h_cm[:, :, 15:16], hlast[:, :, 127:128]),
                    reads=[d_full[:]], writes=[h_cm[:]])

            def p1_head(n):
                xb = x_bufs[n % 2]
                ht = h_toks[0]
                for a in range(4):
                    dma(xb[:, a, :], X_t[:, 4 * n + a, :])
                    ln_stats_tile(xb, xb[:, a, :], a, ht[:])
                ln_finish(4)
                if n > 0:
                    prog.op("dve", lambda s: s.tensor_copy(h_cm[:, :, 15:16], h_cm[:, :, 527:528]),
                            reads=[h_cm[:]], writes=[h_cm[:]])
                for a in range(4):
                    ln_apply(xb, xb[:, a, :], ht[:], a)
                    off = 16 + 128 * a
                    dmat(h_cm[:, :, off:off + 128], ht[:])

            def p1_mixd(n):
                for j in range(CT):
                    prog.op("dve", (lambda s, o=d_full[:, j, :], hj=h_cm[:, j, 16:528],
                                    hsj=h_cm[:, j, 15:527]:
                                    s.tensor_tensor(o, hj, hsj, ALU.subtract)),
                            reads=[h_cm[:]], writes=[d_full[:]], selfsync=True)

            def p1_mixz(n):
                for j in range(CT):
                    hs_j = h_cm[:, j, 15:527]
                    for w in ("g", "r", "k", "v"):
                        prog.op("act", (lambda s, zj=z_b[w][:, j, :], dj=d_full[:, j, :],
                                        mw=m_c[w][:, j:j + 1]:
                                        s.activation(zj, dj, AF.Identity, scale=mw)),
                                reads=[d_full[:], m_c[w][:]], writes=[z_b[w][:]], selfsync=True)
                        prog.op("dve", (lambda s, zj=z_b[w][:, j, :], hsj=hs_j:
                                        s.tensor_tensor(zj, zj, hsj, ALU.add)),
                                reads=[h_cm[:]], writes=[z_b[w][:]], selfsync=True)

            def p1_fill(n):
                xb = x_bufs[n % 2]
                ht = h_toks[0]

                def part1():
                    for a in range(2):
                        dma(xb[:, a, :], X_t[:, 4 * n + a, :])
                        ln_stats_tile(xb, xb[:, a, :], a, ht[:])

                def part2():
                    for a in range(2, 4):
                        dma(xb[:, a, :], X_t[:, 4 * n + a, :])
                        ln_stats_tile(xb, xb[:, a, :], a, ht[:])
                    ln_finish(4)
                    prog.op("dve", lambda s: s.tensor_copy(h_cm[:, :, 15:16], h_cm[:, :, 527:528]),
                            reads=[h_cm[:]], writes=[h_cm[:]])
                    for a in range(4):
                        ln_apply(xb, xb[:, a, :], ht[:], a)
                        off = 16 + 128 * a
                        dmat(h_cm[:, :, off:off + 128], ht[:])

                def part3():
                    p1_mixd(n)
                return {1: part1, 3: part2, 6: part3}

            def p1_proj(n, fill=None):
                pend = []
                for j in range(CT):
                    for w, wn in (("v", "wv"), ("k", "wk"), ("g", "wg"), ("r", "wr")):
                        for k in range(CT):
                            prog.op("pe", (lambda s, ps=psP[w][:], lt=wT_b[wn][:, k, 128 * j:128 * (j + 1)],
                                           rh=z_b[w][:, k, :], st=(k == 0), sp=(k == CT - 1):
                                           s.matmul(ps, lt, rh, start=st, stop=sp)),
                                    reads=[wT_b[wn][:], z_b[w][:]],
                                    writes=[psP[w][:]] if k in (0, CT - 1) else [])
                    prog.op("act", (lambda s, ps=psP["v"][:], b=b_c["v"][:, j:j + 1]:
                                    s.activation(v_t[:], ps, AF.Identity, bias=b)),
                            reads=[psP["v"][:], b_c["v"][:]], writes=[v_t[:]])
                    prog.op("dve", (lambda s, ps=psP["k"][:], b=b_c["k"][:, j:j + 1]:
                                    s.scalar_tensor_tensor(kv_t[:], ps, b, v_t[:], ALU.add, ALU.mult)),
                            reads=[psP["k"][:], b_c["k"][:], v_t[:]], writes=[kv_t[:]])
                    prog.op("act", (lambda s, o=sg_full[:, j, :], ps=psP["g"][:], b=b_c["g"][:, j:j + 1]:
                                    s.activation(o, ps, AF.Sigmoid, bias=b)),
                            reads=[psP["g"][:], b_c["g"][:]], writes=[sg_full[:]])
                    prog.op("act", (lambda s, ps=psP["r"][:], b=b_c["r"][:, j:j + 1]:
                                    s.activation(r_t[:], ps, AF.Identity, bias=b)),
                            reads=[psP["r"][:], b_c["r"][:]], writes=[r_t[:]])
                    prog.op("dve", (lambda s, d0=decay_c[:, j:j + 1].broadcast_to([128, CHT]),
                                    init=carry[:, j:j + 1]:
                                    s.tensor_tensor_scan(wkv_t[:], d0, kv_t[:], init, ALU.mult, ALU.add)),
                            reads=[decay_c[:], kv_t[:], carry[:]], writes=[wkv_t[:]])
                    prog.op("dve", (lambda s, cc=carry[:, j:j + 1]:
                                    s.tensor_copy(cc, wkv_t[:, CHT - 1:CHT])),
                            reads=[wkv_t[:]], writes=[carry[:]], selfsync=True)
                    prog.op("dve", (lambda s, u=u_c[:, j:j + 1]:
                                    s.scalar_tensor_tensor(wkv_t[:], kv_t[:], u, wkv_t[:], ALU.mult, ALU.add)),
                            reads=[kv_t[:], u_c[:], wkv_t[:]], writes=[wkv_t[:]])
                    prog.op("pool", (lambda s, o=rw_full[:, j, :]:
                                     s.tensor_tensor(o, r_t[:], wkv_t[:], ALU.mult)),
                            reads=[r_t[:], wkv_t[:]], writes=[rw_full[:]])
                    prog.op("pool", (lambda s, o=rw2_t[j % 2][:], i_=rw_full[:, j, :]:
                                     s.tensor_tensor(o, i_, i_, ALU.mult)),
                            reads=[rw_full[:]], writes=[rw2_t[j % 2][:]],
                            selfsync=True)
                    pend.append(j)
                    if len(pend) > 1:
                        emit_stat_mm(pend.pop(0))
                    if fill and j in fill:
                        fill[j]()
                while pend:
                    emit_stat_mm(pend.pop(0))

            def emit_stat_mm(j):
                prog.op("pe", (lambda s, lt=gnsel[:, j, :], i_=rw_full[:, j, :],
                               st=(j == 0), sp=(j == CT - 1):
                               s.matmul(psSum[:], lt, i_, start=st, stop=sp)),
                        reads=[gnsel[:], rw_full[:]],
                        writes=[psSum[:]] if j in (0, CT - 1) else [])
                prog.op("pe", (lambda s, lt=gnsel[:, j, :], i_=rw2_t[j % 2][:],
                               st=(j == 0), sp=(j == CT - 1):
                               s.matmul(psSsq[:], lt, i_, start=st, stop=sp)),
                        reads=[gnsel[:], rw2_t[j % 2][:]],
                        writes=[psSsq[:]] if j in (0, CT - 1) else [])

            def p1_gn(n):
                prog.op("dve", lambda s: s.tensor_scalar(gmean[:], psSum[:], 1.0 / 64, None, ALU.mult),
                        reads=[psSum[:]], writes=[gmean[:]])
                prog.op("dve", lambda s: s.tensor_tensor(gt1[:], gmean[:], gmean[:], ALU.mult),
                        reads=[gmean[:]], writes=[gt1[:]])
                prog.op("dve", lambda s: s.scalar_tensor_tensor(gvar[:], psSsq[:], 1.0 / 64, gt1[:], ALU.mult, ALU.subtract),
                        reads=[psSsq[:], gt1[:]], writes=[gvar[:]])
                prog.op("dve", lambda s: s.tensor_scalar(gvar[:], gvar[:], EPS, None, ALU.add),
                        reads=[gvar[:]], writes=[gvar[:]])
                quake_rsqrt("dve", gvar[:], grstd[:], gt1[:],
                            gvar[:], grstd[:], gt1[:])
                gmean_bf = gvar[:].bitcast(BF16)[:, :CHT]
                grstd_bf = gt1[:].bitcast(BF16)[:, :CHT]
                prog.op("dve", lambda s, o=gmean_bf: s.tensor_copy(o, gmean[:]),
                        reads=[gmean[:]], writes=[gvar[:]], selfsync=True)
                prog.op("dve", lambda s, o=grstd_bf: s.tensor_copy(o, grstd[:]),
                        reads=[grstd[:]], writes=[gt1[:]], selfsync=True)
                for j in range(CT):
                    prog.op("pe", (lambda s, lt=gnbsel[:, j, :], rh=gmean_bf:
                                   s.matmul(psMu[:], lt, rh, start=True, stop=True)),
                            reads=[gnbsel[:], gvar[:]], writes=[psMu[:]])
                    prog.op("pe", (lambda s, lt=gnbsel[:, j, :], rh=grstd_bf:
                                   s.matmul(psRs[:], lt, rh, start=True, stop=True)),
                            reads=[gnbsel[:], gt1[:]], writes=[psRs[:]])
                    prog.op("dve", (lambda s, rw=rw_full[:, j, :]:
                                    s.tensor_tensor(kv_t[:], rw, psMu[:], ALU.subtract)),
                            reads=[rw_full[:], psMu[:]], writes=[kv_t[:]])
                    prog.op("dve", lambda s: s.tensor_tensor(wkv_t[:], kv_t[:], psRs[:], ALU.mult),
                            reads=[kv_t[:], psRs[:]], writes=[wkv_t[:]])
                    prog.op("dve", (lambda s, o=inner[:, j, :], sg=sg_full[:, j, :]:
                                    s.tensor_tensor(o, wkv_t[:], sg, ALU.mult)),
                            reads=[wkv_t[:], sg_full[:]], writes=[inner[:]])

            WO_ROT = ("g", "r", "k", "v")

            def p1_wo(n):
                xb = x_bufs[n % 2]

                def res1(i):
                    a, hh = divmod(i, 2)
                    ps = psP[WO_ROT[i % 4]]
                    prog.op("dve", (lambda s, ps_=ps[:], x=xb[:, a, 512 * hh:512 * (hh + 1)]:
                                    s.tensor_tensor(x, x, ps_, ALU.add)),
                            reads=[xb[:], ps[:]], writes=[xb[:]])
                    if hh == 1:
                        dma(OUT_t[:, 4 * n + a, :], xb[:, a, :])

                for i in range(8):
                    a, hh = divmod(i, 2)
                    ps = psP[WO_ROT[i % 4]]
                    for j in range(CT):
                        prog.op("pe", (lambda s, ps_=ps[:], lt=inner[:, j, 128 * a:128 * (a + 1)],
                                       rh=wT_b["wo"][:, j, 512 * hh:512 * (hh + 1)], st=(j == 0):
                                       s.matmul(ps_, lt, rh, start=st, stop=False)),
                                reads=[inner[:], wT_b["wo"][:]],
                                writes=[ps[:]] if j == 0 else [])
                    prog.op("pe", (lambda s, ps_=ps[:], rh=bo_bf[0:1, 512 * hh:512 * (hh + 1)]:
                                   s.matmul(ps_, ones1[:], rh, start=False, stop=True)),
                            reads=[ones1[:], bo_bf[:]], writes=[ps[:]])
                    if i >= 3:
                        res1(i - 3)
                for i in range(5, 8):
                    res1(i)

            p1_head(0)
            p1_mixd(0)
            p1_mixz(0)
            for n in range(NCH):
                p1_proj(n, p1_fill(n + 1) if n + 1 < NCH else None)
                if n + 1 < NCH:
                    p1_mixz(n + 1)
                p1_gn(n)
                p1_wo(n)

            prog.barrier()

        if stop_after >= 2:
          with ExitStack() as pB:
            def sbB(name, shape, dtype=F32):
                return pB.enter_context(nc.sbuf_tensor(name, shape, dtype))
            wgT2 = [sbB(f"wgT2_{j}", [128, II], BF16) for j in range(CT)]
            wuT2 = [sbB(f"wuT2_{j}", [128, II], BF16) for j in range(CT)]
            x1_buf = sbB("x1_buf", [128, 4, C], F32)
            x1_buf2 = sbB("x1_buf2", [128, 4, C], F32)
            x2_t2 = [sbB("x2t2_0", [128, C], F32)]
            qb2p = {"x": tuple(sbB(f"qb2{nm}x", [128, C], F16) for nm in "AMH"),
                    "h": tuple(sbB(f"qb2{nm}h", [128, II], F16) for nm in "AMH")}
            qb2R = {"x": [sbB(f"qb2R{i}x", [128, C], F16) for i in range(2)],
                    "h": [sbB(f"qb2R{i}h", [128, II], F16) for i in range(2)]}
            qsc2p = {"x": sbB("qsc2_x", [128, C // 16]),
                     "h": sbB("qsc2_h", [128, II // 16])}
            qrs2p = {"x": sbB("qrs2_x", [128, C // 16]),
                     "h": sbB("qrs2_h", [128, II // 16])}
            xq_cm = [sbB(f"xq_cm{i}", [128, 8, 128], BF16) for i in range(2)]
            sil_t = [sbB(f"sil_t{i}", [128, 512], F16) for i in range(2)]
            up_t = [sbB(f"up_t{i}", [128, 512], F16) for i in range(2)]
            h16 = [sbB(f"h16_{i}", [128, II], F16) for i in range(3)]
            l2sum = sbB("l2sum", [128, 8])
            l2ssq = sbB("l2ssq", [128, 8])
            l2mean = sbB("l2mean", [128, 8])
            l2var = sbB("l2var", [128, 8])
            l2t1 = sbB("l2t1", [128, 8])
            l2rstd = sbB("l2rstd", [128, 8])
            l2nb = sbB("l2nb", [128, 8])
            l2ssqB = sbB("l2ssqB", [128, 8])
            l2nbs = sbB("l2nbs", [128, 8])
            l2rstds = sbB("l2rstds", [128, 8])
            psGt = [pB.enter_context(nc.psum_tensor(f"psGt{i}", [128, 512], F32))
                    for i in range(2)]
            psUp = [pB.enter_context(nc.psum_tensor(f"psUp{i}", [128, 512], F32))
                    for i in range(2)]

            prog.op("act", lambda s: s.preload_act_table(AF.Silu))


            def emit_quant2(src_ap, src_full, width, par, bi):
                bA, bM, bH = qb2p[par]
                bR = qb2R[par][bi]
                qsc2 = qsc2p[par]
                qrs2 = qrs2p[par]
                nb = width // 16
                sc = qsc2[:, :nb]; rs = qrs2[:, :nb]
                src3 = src_ap.rearrange("p (b k) -> p b k", k=16)
                prog.op("dve", (lambda s, o=sc, x=src3: s.tensor_reduce(
                    o, x, AX.X, ALU.max, apply_absolute_value=True)),
                    reads=[src_full], writes=[qsc2[:]])
                prog.op("dve", (lambda s, o=sc: s.tensor_scalar(
                    o, o, 1.0 / 6.0, 1e-12, ALU.mult, ALU.max)),
                    reads=[qsc2[:]], writes=[qsc2[:]], selfsync=True)
                prog.op("dve", (lambda s, o=rs, i_=sc: s.reciprocal(o, i_)),
                        reads=[qsc2[:]], writes=[qrs2[:]], selfsync=True)
                xn = bA[:, :width]
                rsb = rs.unsqueeze(2).broadcast_to([128, nb, 16])
                scb = sc.unsqueeze(2).broadcast_to([128, nb, 16])
                xn3 = xn.rearrange("p (b k) -> p b k", k=16)
                prog.op("dve", (lambda s, o=xn3, x=src3, r=rsb:
                                s.tensor_tensor(o, x, r, ALU.mult)),
                        reads=[src_full, qrs2[:]], writes=[bA[:]])
                t = xn.bitcast(U16)
                br = bR[:, :width].bitcast(U16)
                bm = bM[:, :width].bitcast(U16)
                bh = bH[:, :width].bitcast(U16)
                prog.op("dve", (lambda s: s.tensor_single_scalar(bm, t, 0x7FFF, ALU.bitwise_and)),
                        reads=[bA[:]], writes=[bM[:]])
                prog.op("dve", (lambda s: s.tensor_single_scalar(br, t, 0x0100, ALU.add)),
                        reads=[bA[:]], writes=[bR[:]])
                prog.op("dve", (lambda s: s.tensor_single_scalar(br, br, 0x7E00, ALU.bitwise_and)),
                        reads=[bR[:]], writes=[bR[:]], selfsync=True)
                prog.op("dve", (lambda s: s.tensor_scalar(br, br, 0x3C00, 0x3800, ALU.max, ALU.subtract)),
                        reads=[bR[:]], writes=[bR[:]], selfsync=True)
                prog.op("dve", (lambda s: s.tensor_scalar(bh, bm, 0x3400, 0x3800, ALU.is_gt, ALU.mult)),
                        reads=[bM[:]], writes=[bH[:]])
                prog.op("dve", (lambda s: s.scalar_tensor_tensor(br, bm, 0x3A00, br, ALU.is_gt, ALU.mult)),
                        reads=[bM[:], bR[:]], writes=[bR[:]])
                prog.op("dve", (lambda s: s.tensor_tensor(br, br, bh, ALU.add)),
                        reads=[bR[:], bH[:]], writes=[bR[:]], selfsync=True)
                prog.op("dve", (lambda s: s.tensor_single_scalar(bm, t, 0x8000, ALU.bitwise_and)),
                        reads=[bA[:]], writes=[bM[:]])
                prog.op("dve", (lambda s: s.tensor_tensor(br, bm, br, ALU.bitwise_or)),
                        reads=[bM[:], bR[:]], writes=[bR[:]])
                q3 = bR[:, :width].rearrange("p (b k) -> p b k", k=16)
                ob3 = bR[:, :width].bitcast(BF16).rearrange("p (b k) -> p b k", k=16)
                prog.op("pool", (lambda s, o=ob3, q=q3, ss=scb:
                                 s.tensor_tensor(o, q, ss, ALU.mult)),
                        reads=[bR[:], qsc2[:]], writes=[bR[:]])

            WG4 = WGall.rearrange("p j k r -> p j (k r)")
            WU4 = WUall.rearrange("p j k r -> p j (k r)")

            IC = [(0, 512), (512, 512), (1024, 512), (1536, 512), (2048, 512), (2560, 256)]

            def ln2_finish(ncols):
                sl = slice(0, ncols)
                ms = l2mean[:, sl]; vs = l2var[:, sl]
                t1 = l2t1[:, sl]
                ys = l2rstd[:, sl]; nb_ = l2nb[:, sl]
                prog.op("dve", (lambda s: s.tensor_scalar(ms, l2sum[:, sl], 1.0 / C, None, ALU.mult)),
                        reads=[l2sum[:]], writes=[l2mean[:]], selfsync=True)
                prog.op("dve", (lambda s: s.tensor_tensor(t1, ms, ms, ALU.mult)),
                        reads=[l2mean[:]], writes=[l2t1[:]], selfsync=True)
                prog.op("dve", (lambda s: s.scalar_tensor_tensor(vs, l2ssq[:, sl], 1.0 / C, t1, ALU.mult, ALU.subtract)),
                        reads=[l2ssq[:], l2t1[:]], writes=[l2var[:]], selfsync=True)
                prog.op("dve", (lambda s: s.tensor_scalar(vs, vs, EPS, None, ALU.add)),
                        reads=[l2var[:]], writes=[l2var[:]], selfsync=True)
                quake_rsqrt("dve", vs, ys, t1, l2var[:], l2rstd[:], l2t1[:])
                prog.op("dve", (lambda s: s.scalar_tensor_tensor(nb_, ms, -1.0, ys, ALU.mult, ALU.mult)),
                        reads=[l2mean[:], l2rstd[:]], writes=[l2nb[:]], selfsync=True)

            x1_bufs = [x1_buf, x1_buf2]

            def p2a_load(n):
                for a in range(4):
                    dma(x1_bufs[n % 2][:, a, :], OUT_t[:, 4 * n + a, :])

            def p2a_prep(n):
                xb = x1_bufs[n % 2]
                for aa in range(4):
                    prog.op("dve", (lambda s, o=l2sum[:, aa:aa + 1], x=xb[:, aa, :]:
                                    s.reduce_sum(o, x, AX.X)),
                            reads=[xb[:]], writes=[l2sum[:]])
                    jk = x2_t2[0]
                    prog.op("act", (lambda s, x=xb[:, aa, :512], acc=l2ssq[:, aa:aa + 1],
                                    o=jk[:, :512]:
                                    s.activation(o, x, AF.Square, accum_out=acc)),
                            reads=[xb[:]], writes=[jk[:], l2ssq[:]])
                    prog.op("act", (lambda s, x=xb[:, aa, 512:], acc=l2ssqB[:, aa:aa + 1],
                                    o=jk[:, 512:]:
                                    s.activation(o, x, AF.Square, accum_out=acc)),
                            reads=[xb[:]], writes=[jk[:], l2ssqB[:]])
                    prog.op("dve", (lambda s, o=l2ssq[:, aa:aa + 1], b=l2ssqB[:, aa:aa + 1]:
                                    s.tensor_tensor(o, o, b, ALU.add)),
                            reads=[l2ssq[:], l2ssqB[:]], writes=[l2ssq[:]],
                            selfsync=True)
                ln2_finish(4)
                prog.op("dve", (lambda s, o=l2nbs[:, 4 * (n % 2):4 * (n % 2) + 4]:
                                s.tensor_copy(o, l2nb[:, 0:4])),
                        reads=[l2nb[:]], writes=[l2nbs[:]], selfsync=True)
                prog.op("dve", (lambda s, o=l2rstds[:, 4 * (n % 2):4 * (n % 2) + 4]:
                                s.tensor_copy(o, l2rstd[:, 0:4])),
                        reads=[l2rstd[:]], writes=[l2rstds[:]], selfsync=True)

            def p2a_front(g):
                n, a = divmod(g, 4)
                xb = x1_bufs[n % 2]
                pa = g % 2
                x2a = x2_t2[0]
                col = 4 * (n % 2) + a
                prog.op("act", (lambda s, o=x2a[:], x=xb[:, a, :],
                                b=l2nbs[:, col:col + 1], sc=l2rstds[:, col:col + 1]:
                                s.activation(o, x, AF.Identity, bias=b, scale=sc)),
                        reads=[xb[:], l2nbs[:], l2rstds[:]], writes=[x2a[:]])
                emit_quant2(x2a[:], x2a[:], C, "x", pa)
                xqv = qb2R["x"][pa][:, :C].bitcast(BF16)
                dmat(xq_cm[pa][:], xqv)

            def p2a_mid(g):
                pa = g % 2
                xca = xq_cm[pa]
                h16a = h16[g % 3]
                for ici, (io, iw) in enumerate(IC):
                    pg = psGt[ici % 2]
                    pu = psUp[ici % 2]
                    sl = sil_t[ici % 2]
                    for j in range(CT):
                        prog.op("pe", (lambda s, lt=xca[:, j, :], rh=wgT2[j][:, io:io + iw],
                                       ps=pg[:, :iw], st=(j == 0), sp=(j == CT - 1):
                                       s.matmul(ps, lt, rh, start=st, stop=sp)),
                                reads=[xca[:], wgT2[j][:]],
                                writes=[pg[:]] if j in (0, CT - 1) else [])
                    for j in range(CT):
                        prog.op("pe", (lambda s, lt=xca[:, j, :], rh=wuT2[j][:, io:io + iw],
                                       ps=pu[:, :iw], st=(j == 0), sp=(j == CT - 1):
                                       s.matmul(ps, lt, rh, start=st, stop=sp)),
                                reads=[xca[:], wuT2[j][:]],
                                writes=[pu[:]] if j in (0, CT - 1) else [])
                    prog.op("act", (lambda s, o=sl[:, :iw], ps=pg[:, :iw]:
                                    s.activation(o, ps, AF.Silu)),
                            reads=[pg[:]], writes=[sl[:]])
                    up = up_t[ici % 2]
                    prog.op("act", (lambda s, o=up[:, :iw], ps=pu[:, :iw]:
                                    s.activation(o, ps, AF.Identity)),
                            reads=[pu[:]], writes=[up[:]])
                    prog.op("pool", (lambda s, o=h16a[:, io:io + iw], si=sl[:, :iw], u=up[:, :iw]:
                                     s.tensor_tensor(o, si, u, ALU.mult)),
                            reads=[sl[:], up[:]], writes=[h16a[:]])

            hq_pend = []

            def p2a_back(g):
                while hq_pend:
                    g2 = hq_pend.pop(0)
                    dmac(HQ_t[:, g2, :], qb2R["h"][g2 % 2][:, :II].bitcast(BF16))
                emit_quant2(h16[g % 3][:], h16[g % 3][:], II, "h", g % 2)
                hq_pend.append(g)

            p2a_load(0)
            p2a_prep(0)
            for j in range(CT):
                dmac(wgT2[j][:], WG4[:, j, :])
                dmac(wuT2[j][:], WU4[:, j, :])
            p2a_front(0)
            for g in range(NT):
                if g % 4 == 1 and g // 4 + 1 < NCH:
                    p2a_load(g // 4 + 1)
                if g % 4 == 3 and g // 4 + 1 < NCH:
                    p2a_prep(g // 4 + 1)
                if g + 1 < NT:
                    p2a_front(g + 1)
                p2a_mid(g)
                if g >= 2:
                    p2a_back(g - 2)
            p2a_back(NT - 2)
            p2a_back(NT - 1)
            while hq_pend:
                g2 = hq_pend.pop(0)
                dmac(HQ_t[:, g2, :], qb2R["h"][g2 % 2][:, :II].bitcast(BF16))

            prog.barrier()

        if stop_after >= 3:
          with ExitStack() as pC:
            def sbC(name, shape, dtype=F32):
                return pC.enter_context(nc.sbuf_tensor(name, shape, dtype))
            wdT = sbC("wdT", [128, IT, C], BF16)
            hqT = [sbC(f"hqT{i}", [128, IT, 4, 128], BF16) for i in range(2)]
            x1b = [sbC(f"x1b{i}", [128, 4, C], F32) for i in range(2)]
            psD = [pC.enter_context(nc.psum_tensor(f"psD{i}", [128, 512], F32))
                   for i in range(8)]

            hstg = [sbC(f"hstg{i}", [128, II], BF16) for i in range(2)]
            dmac(wdT[:], WDall.rearrange("p m k r -> p m (k r)"))

            def p2b_prefetch(n):
                for a in range(4):
                    stg = hstg[a % 2]
                    dmac(stg[:], HQd[512 * n + 128 * a:512 * n + 128 * (a + 1), :])
                    dmat(hqT[n % 2][:, :, a, :], stg[:])
                for a in range(4):
                    dma(x1b[n % 2][:, a, :], OUT_t[:, 4 * n + a, :])

            p2b_prefetch(0)
            for n in range(NCH):
                hq_n = hqT[n % 2]
                x1_n = x1b[n % 2]
                for m in range(IT):
                    for a in range(4):
                        for hh in range(2):
                            prog.op("pe", (lambda s, ps=psD[a * 2 + hh][:],
                                           lt=hq_n[:, m, a, :],
                                           rh=wdT[:, m, 512 * hh:512 * (hh + 1)],
                                           st=(m == 0), sp=(m == IT - 1):
                                           s.matmul(ps, lt, rh, start=st, stop=sp)),
                                    reads=[hq_n[:], wdT[:]],
                                    writes=[psD[a * 2 + hh][:]] if m in (0, IT - 1) else [])
                if n + 1 < NCH:
                    p2b_prefetch(n + 1)
                for a in range(4):
                    for hh in range(2):
                        prog.op("dve", (lambda s, x=x1_n[:, a, 512 * hh:512 * (hh + 1)],
                                        ps=psD[a * 2 + hh][:]:
                                        s.tensor_tensor(x, x, ps, ALU.add)),
                                reads=[x1_n[:], psD[a * 2 + hh][:]], writes=[x1_n[:]])
                    dma(OUT_t[:, 4 * n + a, :], x1_n[:, a, :])

            prog.barrier()

        with nc.Block("main") as blk:
            prog.emit_all(blk, sems)

    return nc


_NC_CACHE = None


def kernel(**inputs):
    global _NC_CACHE
    f32 = np.float32
    bf16 = ml_dtypes.bfloat16
    x = np.ascontiguousarray(inputs["L0_attn_x"], dtype=f32)

    def colmaj(v):
        return np.ascontiguousarray(np.asarray(v, f32).reshape(8, 128).T)

    base = {}
    for nm, wk in (("wg", "Wg"), ("wr", "Wr"), ("wk", "Wk"), ("wv", "Wv"), ("wo", "Wo")):
        base[f"{nm}T"] = np.ascontiguousarray(
            np.asarray(inputs[f"L0_attn_{wk}_w"], f32).T)
    for nm, wk in (("g", "Wg"), ("r", "Wr"), ("k", "Wk"), ("v", "Wv")):
        base[f"b{nm}"] = colmaj(inputs[f"L0_attn_{wk}_b"])
    base["bo_row"] = np.ascontiguousarray(
        np.asarray(inputs["L0_attn_Wo_b"], f32).reshape(1, C))
    for nm in ("g", "r", "k", "v"):
        base[f"m{nm}"] = colmaj(inputs[f"L0_attn_m{nm}"])
    base["w_vec"] = colmaj(np.asarray(inputs["L0_attn_w"], f32).reshape(-1))
    base["u_vec"] = colmaj(np.asarray(inputs["L0_attn_u"], f32).reshape(-1))
    base["zeros_c"] = np.zeros((128, 8), f32)
    base["ones1"] = np.ones((1, 128), bf16)
    base["ident"] = np.eye(128, dtype=bf16)

    gnsel = np.zeros((128, 8, 16), bf16)
    gnbsel = np.zeros((16, 8, 128), bf16)
    for j in range(8):
        for p in range(128):
            hh = 2 * j + p // 64
            gnsel[p, j, hh] = 1
            gnbsel[hh, j, p] = 1
    base["gnsel"] = np.ascontiguousarray(gnsel.reshape(128, 8 * 16))
    base["gnbsel"] = np.ascontiguousarray(gnbsel.reshape(16, 8 * 128))
    base["w_gate"] = np.ascontiguousarray(inputs["L0_ffn_w_gate"], dtype=f32)
    base["w_up"] = np.ascontiguousarray(inputs["L0_ffn_w_up"], dtype=f32)
    base["w_down"] = np.ascontiguousarray(inputs["L0_ffn_w_down"], dtype=f32)

    if _NC_CACHE is None:
        _NC_CACHE = build_nc()
    nc = _NC_CACHE

    in_maps = []
    for b in range(8):
        m = dict(base)
        m["x"] = np.ascontiguousarray(x[b])
        in_maps.append(m)

    res = run_bass_kernel_spmd(nc, in_maps, core_ids=list(range(8)))
    out = np.stack([np.asarray(res.results[b]["out"]) for b in range(8)], axis=0)
    return out.astype(np.float32)
